# version 02
# baseline (speedup 1.0000x reference)
import numpy as np
import ml_dtypes
from contextlib import ExitStack

import concourse.bass as bass
import concourse.mybir as mybir
from concourse.bass_utils import run_bass_kernel_spmd

F32 = mybir.dt.float32
BF16 = mybir.dt.bfloat16
AF = mybir.ActivationFunctionType
ALU = mybir.AluOpType

D = 2048
T = 2048
DC = 16
NH = 16
HF = 8192
HFC = 64
TG = 512
NTG = 4
LN_EPS = 1e-5
ZA_EPS = 1e-6
EXP_SHIFT = 12.0
ATTN_SCALE = float(1.0 / np.sqrt(128.0))
W2_SCALE = float(1.0 / np.sqrt(8192.0))

ENGS = ("tensor", "vector", "scalar", "gpsimd", "sync")
N_SUB = 4


class Sems:
    def __init__(self, nc, stack):
        self.h = {}
        self.n = {}
        for nm in ("w", "r", "o", "pe", "dv", "ac", "pl"):
            self.h[nm] = stack.enter_context(nc.semaphore("sem_" + nm))
            self.n[nm] = 0

    def at(self, nm):
        return self.n[nm]


class Ph:
    def __init__(self, nc):
        self.nc = nc
        self.q = {e: [] for e in ENGS}

    def run(self):
        with self.nc.Block() as block:
            for eng in ENGS:
                ops = self.q[eng]
                if ops:
                    getattr(block, eng)(lambda e, ops=ops: [f(e) for f in ops])


class B:
    def __init__(self, nc, S):
        self.nc = nc
        self.S = S
        self.ph = None
        self.guard = None
        self.first = None

    def new_phase(self):
        self.ph = Ph(self.nc)
        self.guard = {nm: self.S.at(nm) for nm in ("pe", "dv", "ac", "pl", "r", "w", "o")}
        self.first = {e: True for e in ENGS}
        return self.ph

    def _waits(self, eng, waits):
        S = self.S
        wl = list(waits)
        if self.first[eng]:
            self.first[eng] = False
            wl = list(self.guard.items()) + wl
        agg = {}
        for nm, v in wl:
            if v > 0:
                agg[nm] = max(agg.get(nm, 0), v)
        return tuple((S.h[nm], v) for nm, v in agg.items())

    def emit(self, eng, fn, waits=(), inc=None, k=1):
        S = self.S
        winfo = self._waits(eng, waits)
        h = None
        if inc is not None:
            h = S.h[inc]
            S.n[inc] += k
        def g(e, winfo=winfo, fn=fn, h=h, k=k):
            for hh, v in winfo:
                e.wait_ge(hh, v)
            ins = fn(e)
            if h is not None:
                ins.then_inc(h, k)
        self.ph.q[eng].append(g)

    def emit_dmas(self, eng, fns, waits=(), sem="w"):
        S = self.S
        winfo = self._waits(eng, waits)
        h = S.h[sem]
        S.n[sem] += 16 * len(fns)
        def g(e, winfo=winfo, fns=tuple(fns), h=h):
            for hh, v in winfo:
                e.wait_ge(hh, v)
            for fn in fns:
                fn(e).then_inc(h, 16)
        self.ph.q[eng].append(g)


def _r3(hbm_2d):
    return hbm_2d.rearrange("(dc p) t -> p dc t", p=128)



def ln_phase(b, nc, S, cons, src_hbm, LNX, name):
    ones_bf = cons["ones_bf"]
    with ExitStack() as st:
        XS = [st.enter_context(nc.sbuf_tensor(f"{name}_xs{i}", [128, DC, TG], F32)) for i in range(2)]
        XB = [st.enter_context(nc.sbuf_tensor(f"{name}_xb{i}", [128, TG], BF16)) for i in range(2)]
        XQ = [st.enter_context(nc.sbuf_tensor(f"{name}_xq{i}", [128, TG], BF16)) for i in range(2)]
        M0 = st.enter_context(nc.sbuf_tensor(f"{name}_m0", [128, TG], F32))
        M = [M0, M0]
        MR = [st.enter_context(nc.sbuf_tensor(f"{name}_mr{i}", [128, TG], F32)) for i in range(2)]
        R = [st.enter_context(nc.sbuf_tensor(f"{name}_rr{i}", [128, TG], F32)) for i in range(2)]
        T1 = st.enter_context(nc.sbuf_tensor(f"{name}_t1", [128, TG], F32))
        T2 = st.enter_context(nc.sbuf_tensor(f"{name}_t2", [128, TG], F32))
        P1 = [T1, T2]
        PSS = [st.enter_context(nc.psum_tensor(f"{name}_pss{i}", [128, TG], F32)) for i in range(2)]
        PSQ = [st.enter_context(nc.psum_tensor(f"{name}_psq{i}", [128, TG], F32)) for i in range(2)]
        b.new_phase()
        src3 = _r3(src_hbm)
        apply_done = [0, 0]
        stats_read = [0, 0]
        sub_done = [0, 0]
        sq_read = [0, 0]
        cast_read = [0, 0]
        for tg in range(NTG):
            sl = tg % 2
            b.emit_dmas("sync", [lambda e, sl=sl, tg=tg: e.dma_start(
                XS[sl][:], src3[:, :, tg * TG:(tg + 1) * TG])],
                waits=[("dv", apply_done[sl])], sem="r")
            rtag = S.at("r")
            for dc in range(DC):
                bsl = dc % 2
                b.emit("vector", lambda e, sl=sl, dc=dc, bsl=bsl: e.tensor_copy(
                    XB[bsl][:], XS[sl][:, dc, :]),
                    waits=[("r", rtag), ("pe", cast_read[bsl])], inc="dv")
                cast_tag = S.at("dv")
                b.emit("gpsimd", lambda e, sl=sl, dc=dc, bsl=bsl: e.tensor_tensor(
                    XQ[bsl][:], XS[sl][:, dc, :], XS[sl][:, dc, :], ALU.mult),
                    waits=[("r", rtag), ("pe", sq_read[bsl])], inc="pl")
                sq_tag = S.at("pl")
                def mm(e, sl=sl, dc=dc, bsl=bsl):
                    e.matmul(PSS[sl][:], ones_bf[:], XB[bsl][:],
                             start=(dc == 0), stop=(dc == DC - 1))
                    return e.matmul(PSQ[sl][:], ones_bf[:], XQ[bsl][:],
                                    start=(dc == 0), stop=(dc == DC - 1))
                wl = [("dv", cast_tag), ("pl", sq_tag)]
                if dc == 0:
                    wl.append(("dv", stats_read[sl]))
                b.emit("tensor", mm, waits=wl, inc="pe", k=2)
                cast_read[bsl] = S.at("pe")
                sq_read[bsl] = S.at("pe")
            petag = S.at("pe")
            def stats1(e, sl=sl):
                e.tensor_scalar_mul(M[sl][:], PSS[sl][:], 1.0 / D)
                e.tensor_scalar_mul(T1[:], PSQ[sl][:], 1.0 / D)
                e.tensor_tensor(T2[:], M[sl][:], M[sl][:], ALU.mult)
                return e.tensor_tensor(T1[:], T1[:], T2[:], ALU.subtract)
            b.emit("vector", stats1, waits=[("pe", petag)], inc="dv")
            stats_read[sl] = S.at("dv")
            b.emit("scalar", lambda e: e.activation(T2[:], T1[:], AF.Sqrt, bias=cons["eps_col"]),
                   waits=[("dv", S.at("dv"))], inc="ac")
            def stats2(e, sl=sl):
                e.reciprocal(R[sl][:], T2[:])
                return e.tensor_tensor(MR[sl][:], M[sl][:], R[sl][:], ALU.mult)
            b.emit("vector", stats2, waits=[("ac", S.at("ac"))], inc="dv")
            dv_stats2 = S.at("dv")
            for dc in range(DC):
                bsl = dc % 2
                b.emit("gpsimd", lambda e, sl=sl, dc=dc, bsl=bsl: e.tensor_tensor(
                    P1[bsl][:], XS[sl][:, dc, :], R[sl][:], ALU.mult),
                    waits=[("dv", max(dv_stats2, sub_done[bsl]))], inc="pl")
                pltag = S.at("pl")
                b.emit("vector", lambda e, sl=sl, dc=dc, bsl=bsl, tg=tg: e.tensor_tensor(
                    LNX[:, dc, tg * TG:(tg + 1) * TG], P1[bsl][:], MR[sl][:], ALU.subtract),
                    waits=[("pl", pltag)], inc="dv")
                sub_done[bsl] = S.at("dv")
            apply_done[sl] = S.at("dv")
        b.ph.run()
    return S.at("dv")


def oproj_accum(b, nc, S, SRC, WO_t, bias_t, src_hbm, dst_hbm, name, src_ready_tags, wo_tag):
    with ExitStack() as st:
        Ta = [st.enter_context(nc.sbuf_tensor(f"{name}_ta{i}", [128, TG], F32)) for i in range(2)]
        OUTT = [st.enter_context(nc.sbuf_tensor(f"{name}_ot{i}", [128, TG], F32)) for i in range(2)]
        RESO = [st.enter_context(nc.sbuf_tensor(f"{name}_rs{i}", [128, 8, TG], F32)) for i in range(2)]
        PO = [st.enter_context(nc.psum_tensor(f"{name}_po{i}", [128, TG], F32)) for i in range(2)]
        b.new_phase()
        src3 = _r3(src_hbm)
        dst3 = _r3(dst_hbm)
        po_free = [0, 0]
        ta_free = [0, 0]
        out_free = [0, 0]
        reso_free = [0, 0]
        reso_tag = [0, 0]
        for tg in range(NTG):
            for half in range(2):
                b.emit_dmas("sync", [lambda e, tg=tg, half=half: e.dma_start(
                    RESO[half][:], src3[:, 8 * half:8 * (half + 1), tg * TG:(tg + 1) * TG])],
                    waits=[("dv", reso_free[half])], sem="r")
                reso_tag[half] = S.at("r")
            for dc in range(DC):
                psl = dc % 2
                hsl = dc // 8
                def mo(e, tg=tg, dc=dc, psl=psl):
                    for hec in range(16):
                        ins = e.matmul(PO[psl][:], WO_t[:, hec, dc * 128:(dc + 1) * 128],
                                       SRC[:, hec, tg * TG:(tg + 1) * TG],
                                       start=(hec == 0), stop=(hec == 15))
                    return ins
                b.emit("tensor", mo,
                       waits=[("w", wo_tag), ("dv", src_ready_tags[tg]), ("ac", po_free[psl])],
                       inc="pe")
                pet = S.at("pe")
                b.emit("scalar", lambda e, dc=dc, psl=psl: e.activation(
                    Ta[psl][:], PO[psl][:], AF.Identity, bias=bias_t[:, dc:dc + 1]),
                    waits=[("pe", pet), ("dv", ta_free[psl])], inc="ac")
                po_free[psl] = S.at("ac")
                actag = S.at("ac")
                osl = dc % 2
                b.emit("vector", lambda e, dc=dc, psl=psl, osl=osl, hsl=hsl: e.tensor_tensor(
                    OUTT[osl][:], Ta[psl][:], RESO[hsl][:, dc % 8, :], ALU.add),
                    waits=[("ac", actag), ("r", reso_tag[hsl]), ("o", out_free[osl])], inc="dv")
                ta_free[psl] = S.at("dv")
                dvt = S.at("dv")
                if dc == 7:
                    reso_free[0] = S.at("dv")
                if dc == 15:
                    reso_free[1] = S.at("dv")
                b.emit_dmas("sync", [lambda e, tg=tg, dc=dc, osl=osl: e.dma_start(
                    dst3[:, dc, tg * TG:(tg + 1) * TG], OUTT[osl][:])],
                    waits=[("dv", dvt)], sem="o")
                out_free[osl] = S.at("o")
        b.ph.run()
    return S.at("o")


def sub_attn_a(b, nc, S, cons, src_hbm, dst_hbm, awq, awk, awv, awo, abq_t, abk, abv, abo_t):
    ones_bf = cons["ones_bf"]
    ones_f = cons["ones_f"]
    with ExitStack() as big:
        kv_bf = big.enter_context(nc.sbuf_tensor("a1_kvbf", [128, NH * 128], BF16, side="right"))
        ksr = big.enter_context(nc.sbuf_tensor("a1_ksr", [128, NH * 128], BF16, side="right"))
        lnx_stack = ExitStack()
        LNX = lnx_stack.enter_context(nc.sbuf_tensor("a1_lnx", [128, DC, T], BF16))
        ln_phase(b, nc, S, cons, src_hbm, LNX, "ln_a1")

        with ExitStack() as st:
            kv_acc = st.enter_context(nc.sbuf_tensor("a1_kvacc", [128, NH * 128], F32))
            ks_acc = st.enter_context(nc.sbuf_tensor("a1_ksacc", [128, NH], F32))
            B_bk = st.enter_context(nc.sbuf_tensor("a1_bbk", [128, D], F32))
            B_bv = st.enter_context(nc.sbuf_tensor("a1_bbv", [128, D], F32))
            rowb = st.enter_context(nc.sbuf_tensor("a1_rowb", [1, D], F32))
            WKs = [st.enter_context(nc.sbuf_tensor(f"a1_wk{i}", [128, DC, 256], BF16)) for i in range(2)]
            WVs = [st.enter_context(nc.sbuf_tensor(f"a1_wv{i}", [128, DC, 256], BF16)) for i in range(2)]
            KPt = st.enter_context(nc.sbuf_tensor("a1_kpt", [128, 256], BF16))
            VTt = st.enter_context(nc.sbuf_tensor("a1_vtt", [128, 256], BF16))
            Ta = st.enter_context(nc.sbuf_tensor("a1_ta", [128, 256], F32))
            Tb = st.enter_context(nc.sbuf_tensor("a1_tb", [128, 256], F32))
            Tc = st.enter_context(nc.sbuf_tensor("a1_tc", [128, 256], F32))
            Td = st.enter_context(nc.sbuf_tensor("a1_td", [128, 128], F32))
            PK = [st.enter_context(nc.psum_tensor(f"a1_pk{i}", [128, TG], F32)) for i in range(2)]
            PV = [st.enter_context(nc.psum_tensor(f"a1_pv{i}", [128, TG], F32)) for i in range(2)]
            PKV = [st.enter_context(nc.psum_tensor(f"a1_pkv{i}", [128, TG], F32)) for i in range(2)]
            PBC = st.enter_context(nc.psum_tensor("a1_pbc", [128, TG], F32))
            b.new_phase()

            b.emit_dmas("sync", [lambda e: e.dma_start(rowb[:], abk[:])], sem="r")
            rt = S.at("r")
            for dst_t, r_src in ((B_bk, abk), (B_bv, abv)):
                for q in range(NTG):
                    b.emit("tensor", lambda e, q=q: e.matmul(
                        PBC[:], ones_f[:], rowb[:, q * TG:(q + 1) * TG], start=True, stop=True),
                        waits=[("r", rt), ("dv", cons["dv0"]), ("ac", S.at("ac"))], inc="pe")
                    b.emit("scalar", lambda e, q=q, dst_t=dst_t: e.copy(
                        dst_t[:, q * TG:(q + 1) * TG], PBC[:]),
                        waits=[("pe", S.at("pe"))], inc="ac")
                if r_src is abk:
                    b.emit_dmas("sync", [lambda e: e.dma_start(rowb[:], abv[:])],
                                waits=[("ac", S.at("ac"))], sem="r")
                    rt = S.at("r")
            bias_ac = S.at("ac")

            def zacc(e):
                e.memset(kv_acc[:], 0.0)
                return e.memset(ks_acc[:], 0.0)
            b.emit("vector", zacc, inc="dv")
            acc_zero = S.at("dv")

            awk3 = awk.rearrange("(dc p) o -> p dc o", p=128)
            awv3 = awv.rearrange("(dc p) o -> p dc o", p=128)
            proj_free = [0, 0]
            kv_free = [0, 0]
            kpt_free = 0
            w_free = [0, 0]
            for sp in range(8):
                sl = sp % 2
                b.emit_dmas("gpsimd", [
                    lambda e, sp=sp, sl=sl: e.dma_start(WKs[sl][:], awk3[:, :, sp * 256:(sp + 1) * 256]),
                    lambda e, sp=sp, sl=sl: e.dma_start(WVs[sl][:], awv3[:, :, sp * 256:(sp + 1) * 256]),
                ], waits=[("pe", w_free[sl])], sem="w")
                wt = S.at("w")
                for tt in range(DC):
                    psl = tt % 2
                    def mproj(e, sl=sl, tt=tt, psl=psl):
                        for dc in range(DC):
                            e.matmul(PK[psl][:, 0:256], LNX[:, dc, tt * 128:(tt + 1) * 128],
                                     WKs[sl][:, dc, :], start=(dc == 0), stop=(dc == DC - 1))
                            ins = e.matmul(PV[psl][:, 0:256], LNX[:, dc, tt * 128:(tt + 1) * 128],
                                           WVs[sl][:, dc, :], start=(dc == 0), stop=(dc == DC - 1))
                        return ins
                    b.emit("tensor", mproj, waits=[("w", wt), ("dv", proj_free[psl])], inc="pe")
                    if tt == DC - 1:
                        w_free[sl] = S.at("pe")
                    pet = S.at("pe")
                    def elu1(e, sp=sp, psl=psl):
                        e.tensor_tensor(Ta[:], PK[psl][:, 0:256], B_bk[:, sp * 256:(sp + 1) * 256], ALU.add)
                        return e.tensor_scalar_min(Tb[:], Ta[:], 0.0)
                    b.emit("vector", elu1, waits=[("pe", pet), ("ac", bias_ac)], inc="dv")
                    b.emit("scalar", lambda e: e.activation(Tc[:], Tb[:], AF.Exp),
                           waits=[("dv", S.at("dv"))], inc="ac")
                    def elu2(e, psl=psl):
                        e.tensor_scalar_max(Tb[:], Ta[:], 0.0)
                        e.tensor_tensor(KPt[:], Tc[:], Tb[:], ALU.add)
                        return e.tensor_copy(VTt[:], PV[psl][:, 0:256])
                    b.emit("vector", elu2, waits=[("ac", S.at("ac")), ("pe", kpt_free)], inc="dv")
                    proj_free[psl] = S.at("dv")
                    elu_dv = S.at("dv")
                    for hl in range(2):
                        h = sp * 2 + hl
                        def mkv(e, hl=hl):
                            e.matmul(PKV[hl][:, 0:128], KPt[:, hl * 128:(hl + 1) * 128],
                                     VTt[:, hl * 128:(hl + 1) * 128], start=True, stop=True)
                            return e.matmul(PKV[hl][:, 128:129], KPt[:, hl * 128:(hl + 1) * 128],
                                            ones_bf[:, 0:1], start=True, stop=True)
                        b.emit("tensor", mkv, waits=[("dv", max(elu_dv, kv_free[hl]))], inc="pe")
                        def acc(e, h=h, hl=hl):
                            e.tensor_tensor(kv_acc[:, h * 128:(h + 1) * 128],
                                            kv_acc[:, h * 128:(h + 1) * 128], PKV[hl][:, 0:128], ALU.add)
                            return e.tensor_tensor(ks_acc[:, h:h + 1], ks_acc[:, h:h + 1],
                                                   PKV[hl][:, 128:129], ALU.add)
                        b.emit("vector", acc, waits=[("pe", S.at("pe")), ("dv", acc_zero)], inc="dv")
                        kv_free[hl] = S.at("dv")
                    kpt_free = S.at("pe")
            def fin(e):
                for h in range(NH):
                    e.tensor_scalar_mul(Td[:], B_bv[:, h * 128:(h + 1) * 128], ks_acc[:, h:h + 1])
                    e.tensor_tensor(kv_acc[:, h * 128:(h + 1) * 128],
                                    kv_acc[:, h * 128:(h + 1) * 128], Td[:], ALU.add)
                e.tensor_copy(kv_bf[:], kv_acc[:])
                for h in range(NH):
                    ins = e.tensor_scalar_mul(ksr[:, h * 128:(h + 1) * 128], ones_bf[:],
                                              ks_acc[:, h:h + 1])
                return ins
            b.emit("vector", fin, waits=[("dv", S.at("dv"))], inc="dv")
            b.ph.run()
        kv_ready = S.at("dv")

        QP = big.enter_context(nc.sbuf_tensor("a1_qp", [128, NH, T], BF16, side="right"))
        with ExitStack() as st:
            WQs = [st.enter_context(nc.sbuf_tensor(f"a1_wq{i}", [128, DC, 256], BF16)) for i in range(2)]
            Ta = st.enter_context(nc.sbuf_tensor("a1_qta", [128, TG], F32))
            Tb = st.enter_context(nc.sbuf_tensor("a1_qtb", [128, TG], F32))
            Tc = st.enter_context(nc.sbuf_tensor("a1_qtc", [128, TG], F32))
            PQ = [st.enter_context(nc.psum_tensor(f"a1_pq{i}", [128, TG], F32)) for i in range(2)]
            b.new_phase()
            awq3 = awq.rearrange("(dc p) o -> p dc o", p=128)
            q_free = [0, 0]
            w_free = [0, 0]
            it = 0
            for hp in range(8):
                sl = hp % 2
                b.emit_dmas("gpsimd", [lambda e, hp=hp, sl=sl: e.dma_start(
                    WQs[sl][:], awq3[:, :, hp * 256:(hp + 1) * 256])],
                    waits=[("pe", w_free[sl])], sem="w")
                wt = S.at("w")
                for hl in range(2):
                    hc = hp * 2 + hl
                    for tg in range(NTG):
                        psl = it % 2
                        it += 1
                        def mq(e, sl=sl, hl=hl, tg=tg, psl=psl):
                            for dc in range(DC):
                                ins = e.matmul(PQ[psl][:], WQs[sl][:, dc, hl * 128:(hl + 1) * 128],
                                               LNX[:, dc, tg * TG:(tg + 1) * TG],
                                               start=(dc == 0), stop=(dc == DC - 1))
                            return ins
                        b.emit("tensor", mq, waits=[("w", wt), ("ac", q_free[psl])], inc="pe")
                        if hl == 1 and tg == NTG - 1:
                            w_free[sl] = S.at("pe")
                        b.emit("scalar", lambda e, hc=hc, psl=psl: e.activation(
                            Ta[:], PQ[psl][:], AF.Identity, bias=abq_t[:, hc:hc + 1]),
                            waits=[("pe", S.at("pe")), ("dv", S.at("dv"))], inc="ac")
                        q_free[psl] = S.at("ac")
                        b.emit("vector", lambda e: e.tensor_scalar_min(Tb[:], Ta[:], 0.0),
                               waits=[("ac", S.at("ac"))], inc="dv")
                        b.emit("scalar", lambda e: e.activation(Tc[:], Tb[:], AF.Exp),
                               waits=[("dv", S.at("dv"))], inc="ac")
                        def efin(e, hc=hc, tg=tg):
                            e.tensor_scalar_max(Tb[:], Ta[:], 0.0)
                            return e.tensor_tensor(QP[:, hc, tg * TG:(tg + 1) * TG],
                                                   Tc[:], Tb[:], ALU.add)
                        b.emit("vector", efin, waits=[("ac", S.at("ac"))], inc="dv")
            b.ph.run()

        lnx_stack.close()

        with ExitStack() as st:
            WOa = st.enter_context(nc.sbuf_tensor("a1_woa", [128, DC, D], BF16))
            Ta = st.enter_context(nc.sbuf_tensor("a1_bta", [128, TG], F32))
            Tb = st.enter_context(nc.sbuf_tensor("a1_btb", [128, TG], F32))
            PA = [st.enter_context(nc.psum_tensor(f"a1_pa{i}", [128, TG], F32)) for i in range(2)]
            PZ = [st.enter_context(nc.psum_tensor(f"a1_pz{i}", [128, TG], F32)) for i in range(2)]
            b.new_phase()
            awo3 = awo.rearrange("(hec p) o -> p hec o", p=128)
            b.emit_dmas("gpsimd", [lambda e, i=i: e.dma_start(
                WOa[:, 4 * i:4 * (i + 1), :], awo3[:, 4 * i:4 * (i + 1), :]) for i in range(4)],
                sem="w")
            wo_tag = S.at("w")
            az_free = [0, 0]
            attn_tg_done = []
            for tg in range(NTG):
                for h in range(NH):
                    psl = (tg * NH + h) % 2
                    def ma(e, h=h, tg=tg, psl=psl):
                        e.matmul(PA[psl][:], kv_bf[:, h * 128:(h + 1) * 128],
                                 QP[:, h, tg * TG:(tg + 1) * TG], start=True, stop=True)
                        return e.matmul(PZ[psl][:], ksr[:, h * 128:(h + 1) * 128],
                                        QP[:, h, tg * TG:(tg + 1) * TG], start=True, stop=True)
                    b.emit("tensor", ma, waits=[("dv", max(kv_ready, az_free[psl]))], inc="pe")
                    def za(e, h=h, tg=tg, psl=psl):
                        e.tensor_scalar_add(Ta[:], PZ[psl][:], ZA_EPS)
                        e.reciprocal(Tb[:], Ta[:])
                        return e.tensor_tensor(QP[:, h, tg * TG:(tg + 1) * TG],
                                               PA[psl][:], Tb[:], ALU.mult)
                    b.emit("vector", za, waits=[("pe", S.at("pe"))], inc="dv")
                    az_free[psl] = S.at("dv")
                attn_tg_done.append(S.at("dv"))
            b.ph.run()
            last_o = oproj_accum(b, nc, S, QP, WOa, abo_t, src_hbm, dst_hbm,
                                 "a1_op", attn_tg_done, wo_tag)
    return last_o


def sub_ffn(b, nc, S, cons, src_hbm, dst_hbm, w1T, w3T, w2p, hspl, name):
    with ExitStack() as big:
        lnx_stack = ExitStack()
        LNX = lnx_stack.enter_context(nc.sbuf_tensor(f"{name}_lnx", [128, DC, T], BF16))
        ln_phase(b, nc, S, cons, src_hbm, LNX, f"ln_{name}")
        with ExitStack() as st:
            W1s = [st.enter_context(nc.sbuf_tensor(f"{name}_w1{i}", [128, DC, 256], BF16)) for i in range(2)]
            W3s = [st.enter_context(nc.sbuf_tensor(f"{name}_w3{i}", [128, DC, 256], BF16)) for i in range(2)]
            SIt = [st.enter_context(nc.sbuf_tensor(f"{name}_si{i}", [128, TG], F32)) for i in range(2)]
            HID = [st.enter_context(nc.sbuf_tensor(f"{name}_hid{i}", [128, T], BF16)) for i in range(2)]
            PU1 = [st.enter_context(nc.psum_tensor(f"{name}_pu1{i}", [128, TG], F32)) for i in range(2)]
            PU3 = [st.enter_context(nc.psum_tensor(f"{name}_pu3{i}", [128, TG], F32)) for i in range(2)]
            b.new_phase()
            w1T3 = w1T.rearrange("(dc p) h -> p dc h", p=128)
            w3T3 = w3T.rearrange("(dc p) h -> p dc h", p=128)
            w_free = [0, 0]
            pu1_free = [0, 0]
            pu3_free = [0, 0]
            si_free = [0, 0]
            hid_free = [0, 0]
            it = 0
            for sp in range(32):
                sl = sp % 2
                b.emit_dmas("gpsimd", [
                    lambda e, sp=sp, sl=sl: e.dma_start(W1s[sl][:], w1T3[:, :, sp * 256:(sp + 1) * 256]),
                    lambda e, sp=sp, sl=sl: e.dma_start(W3s[sl][:], w3T3[:, :, sp * 256:(sp + 1) * 256]),
                ], waits=[("pe", w_free[sl])], sem="w")
                wt = S.at("w")
                for hl in range(2):
                    hc = sp * 2 + hl
                    hsl = hc % 2
                    for tg in range(NTG):
                        psl = it % 2
                        it += 1
                        def mup(e, sl=sl, hl=hl, tg=tg, psl=psl):
                            for dc in range(DC):
                                e.matmul(PU1[psl][:], W1s[sl][:, dc, hl * 128:(hl + 1) * 128],
                                         LNX[:, dc, tg * TG:(tg + 1) * TG],
                                         start=(dc == 0), stop=(dc == DC - 1))
                            for dc in range(DC):
                                ins = e.matmul(PU3[psl][:], W3s[sl][:, dc, hl * 128:(hl + 1) * 128],
                                               LNX[:, dc, tg * TG:(tg + 1) * TG],
                                               start=(dc == 0), stop=(dc == DC - 1))
                            return ins
                        wl = [("w", wt), ("ac", pu1_free[psl]), ("dv", pu3_free[psl])]
                        if tg == 0:
                            wl.append(("o", hid_free[hsl]))
                        b.emit("tensor", mup, waits=wl, inc="pe")
                        if hl == 1 and tg == NTG - 1:
                            w_free[sl] = S.at("pe")
                        b.emit("scalar", lambda e, psl=psl: e.activation(
                            SIt[psl][:], PU1[psl][:], AF.Silu),
                            waits=[("pe", S.at("pe")), ("dv", si_free[psl])], inc="ac")
                        pu1_free[psl] = S.at("ac")
                        b.emit("vector", lambda e, hsl=hsl, tg=tg, psl=psl: e.tensor_tensor(
                            HID[hsl][:, tg * TG:(tg + 1) * TG], SIt[psl][:], PU3[psl][:], ALU.mult),
                            waits=[("ac", S.at("ac"))], inc="dv")
                        pu3_free[psl] = S.at("dv")
                        si_free[psl] = S.at("dv")
                    b.emit_dmas("sync", [lambda e, hc=hc, hsl=hsl: e.dma_start(
                        hspl[hc], HID[hsl][:])],
                        waits=[("dv", S.at("dv"))], sem="o")
                    hid_free[hsl] = S.at("o")
            b.ph.run()
        up_done_o = S.at("o")
        lnx_stack.close()

        with ExitStack() as st:
            HIDT = st.enter_context(nc.sbuf_tensor(f"{name}_hidt", [128, HFC, 1024], BF16))
            W2s = [st.enter_context(nc.sbuf_tensor(f"{name}_w2{i}", [128, 4, 512], BF16)) for i in range(2)]
            RESO = [st.enter_context(nc.sbuf_tensor(f"{name}_rs{i}", [128, 4, 1024], F32)) for i in range(2)]
            OUTT = [st.enter_context(nc.sbuf_tensor(f"{name}_ot{i}", [128, TG], F32)) for i in range(4)]
            PD = [st.enter_context(nc.psum_tensor(f"{name}_pd{i}", [128, TG], F32)) for i in range(8)]
            b.new_phase()
            src3 = _r3(src_hbm)
            dst3 = _r3(dst_hbm)
            hspl3 = hspl.rearrange("hc p t -> p hc t")
            w2p3 = w2p.rearrange("c p d -> p c d")
            hidt_free = 0
            w2_free = [0, 0]
            reso_free = [0, 0]
            out_free = [0, 0, 0, 0]
            pd_free = 0
            for th in range(2):
                wl = [("pe", hidt_free)]
                if th == 0:
                    wl.append(("o", up_done_o))
                b.emit_dmas("sync", [lambda e, th=th, g=g: e.dma_start(
                    HIDT[:, 8 * g:8 * (g + 1), :],
                    hspl3[:, 8 * g:8 * (g + 1), th * 1024:(th + 1) * 1024]) for g in range(8)],
                    waits=wl, sem="r")
                ht = S.at("r")
                for dhq in range(4):
                    rsl = dhq % 2
                    b.emit_dmas("sync", [lambda e, th=th, dhq=dhq, rsl=rsl: e.dma_start(
                        RESO[rsl][:], src3[:, 4 * dhq:4 * (dhq + 1), th * 1024:(th + 1) * 1024])],
                        waits=[("dv", reso_free[rsl])], sem="r")
                    rtag = S.at("r")
                    wt = 0
                    for hc in range(HFC):
                        wsl = (hc // 4) % 2
                        if hc % 4 == 0:
                            b.emit_dmas("gpsimd", [lambda e, dhq=dhq, hc=hc, wsl=wsl: e.dma_start(
                                W2s[wsl][:], w2p3[:, hc:hc + 4, dhq * 512:(dhq + 1) * 512])],
                                waits=[("pe", w2_free[wsl])], sem="w")
                            wt = S.at("w")
                        def mdn(e, hc=hc, wsl=wsl):
                            for dcl in range(4):
                                for nq in range(2):
                                    ins = e.matmul(PD[dcl * 2 + nq][:],
                                                   W2s[wsl][:, hc % 4, dcl * 128:(dcl + 1) * 128],
                                                   HIDT[:, hc, nq * TG:(nq + 1) * TG],
                                                   start=(hc == 0), stop=(hc == HFC - 1))
                            return ins
                        wl = [("w", wt), ("r", ht)]
                        if hc == 0:
                            wl.append(("dv", pd_free))
                        b.emit("tensor", mdn, waits=wl, inc="pe")
                        if hc % 4 == 3:
                            w2_free[wsl] = S.at("pe")
                    pet = S.at("pe")
                    for dcl in range(4):
                        for nq in range(2):
                            osl = (dcl * 2 + nq) % 4
                            b.emit("vector", lambda e, dcl=dcl, nq=nq, rsl=rsl, osl=osl: e.tensor_tensor(
                                OUTT[osl][:], PD[dcl * 2 + nq][:],
                                RESO[rsl][:, dcl, nq * TG:(nq + 1) * TG], ALU.add),
                                waits=[("pe", pet), ("r", rtag), ("o", out_free[osl])], inc="dv")
                            dvt = S.at("dv")
                            b.emit_dmas("sync", [lambda e, th=th, dhq=dhq, dcl=dcl, nq=nq, osl=osl: e.dma_start(
                                dst3[:, dhq * 4 + dcl, th * 1024 + nq * TG:th * 1024 + (nq + 1) * TG],
                                OUTT[osl][:])],
                                waits=[("dv", dvt)], sem="o")
                            out_free[osl] = S.at("o")
                    reso_free[rsl] = S.at("dv")
                    pd_free = S.at("dv")
                hidt_free = S.at("pe")
            b.ph.run()
    return S.at("o")


def sub_attn_b(b, nc, S, cons, src_hbm, dst_hbm, bwq, bwk, bwv, bwo,
               bbq_t, bbk_t, bbv_t, bbo_t, qspl, kspl, vspl):
    ones_bf = cons["ones_bf"]
    tri_t = cons["tri_t"]
    with ExitStack() as big:
        lnx_stack = ExitStack()
        LNX = lnx_stack.enter_context(nc.sbuf_tensor("b3_lnx", [128, DC, T], BF16))
        ln_phase(b, nc, S, cons, src_hbm, LNX, "ln_b3")
        with ExitStack() as st:
            Ws = [st.enter_context(nc.sbuf_tensor(f"b3_w{i}", [128, DC, 256], BF16)) for i in range(2)]
            WVs = [st.enter_context(nc.sbuf_tensor(f"b3_wv{i}", [128, DC, 512], BF16)) for i in range(2)]
            QS = [st.enter_context(nc.sbuf_tensor(f"b3_qs{i}", [128, TG], BF16)) for i in range(2)]
            PQ = [st.enter_context(nc.psum_tensor(f"b3_pq{i}", [128, TG], F32)) for i in range(2)]
            b.new_phase()
            bwq3 = bwq.rearrange("(dc p) o -> p dc o", p=128)
            bwk3 = bwk.rearrange("(dc p) o -> p dc o", p=128)
            bwv3 = bwv.rearrange("(dc p) o -> p dc o", p=128)
            w_free = [0, 0]
            pq_free = [0, 0]
            qs_free = [0, 0]
            it = 0
            for wsrc, spl, bias_t in ((bwq3, qspl, bbq_t), (bwk3, kspl, bbk_t)):
                for hp in range(8):
                    sl = hp % 2
                    b.emit_dmas("gpsimd", [lambda e, wsrc=wsrc, hp=hp, sl=sl: e.dma_start(
                        Ws[sl][:], wsrc[:, :, hp * 256:(hp + 1) * 256])],
                        waits=[("pe", w_free[sl])], sem="w")
                    wt = S.at("w")
                    for hl in range(2):
                        hc = hp * 2 + hl
                        for tg in range(NTG):
                            psl = it % 2
                            osl = it % 2
                            it += 1
                            def mq(e, sl=sl, hl=hl, tg=tg, psl=psl):
                                for dc in range(DC):
                                    ins = e.matmul(PQ[psl][:], Ws[sl][:, dc, hl * 128:(hl + 1) * 128],
                                                   LNX[:, dc, tg * TG:(tg + 1) * TG],
                                                   start=(dc == 0), stop=(dc == DC - 1))
                                return ins
                            b.emit("tensor", mq, waits=[("w", wt), ("ac", pq_free[psl])], inc="pe")
                            if hl == 1 and tg == NTG - 1:
                                w_free[sl] = S.at("pe")
                            b.emit("scalar", lambda e, hc=hc, psl=psl, osl=osl, bias_t=bias_t: e.activation(
                                QS[osl][:], PQ[psl][:], AF.Identity, bias=bias_t[:, hc:hc + 1]),
                                waits=[("pe", S.at("pe")), ("o", qs_free[osl])], inc="ac")
                            pq_free[psl] = S.at("ac")
                            b.emit_dmas("sync", [lambda e, spl=spl, hc=hc, tg=tg, osl=osl: e.dma_start(
                                spl[hc][:, tg * TG:(tg + 1) * TG], QS[osl][:])],
                                waits=[("ac", S.at("ac"))], sem="o")
                            qs_free[osl] = S.at("o")
            for sp in range(4):
                sl = sp % 2
                b.emit_dmas("gpsimd", [lambda e, sp=sp, sl=sl: e.dma_start(
                    WVs[sl][:], bwv3[:, :, sp * 512:(sp + 1) * 512])],
                    waits=[("pe", w_free[sl])], sem="w")
                wt = S.at("w")
                for tt in range(DC):
                    psl = it % 2
                    osl = it % 2
                    it += 1
                    def mv(e, sl=sl, tt=tt, psl=psl):
                        for dc in range(DC):
                            ins = e.matmul(PQ[psl][:], LNX[:, dc, tt * 128:(tt + 1) * 128],
                                           WVs[sl][:, dc, :], start=(dc == 0), stop=(dc == DC - 1))
                        return ins
                    b.emit("tensor", mv, waits=[("w", wt), ("ac", pq_free[psl])], inc="pe")
                    if tt == DC - 1:
                        w_free[sl] = S.at("pe")
                    b.emit("scalar", lambda e, psl=psl, osl=osl: e.copy(QS[osl][:], PQ[psl][:]),
                           waits=[("pe", S.at("pe")), ("o", qs_free[osl])], inc="ac")
                    pq_free[psl] = S.at("ac")
                    b.emit_dmas("sync", [lambda e, tt=tt, sp=sp, osl=osl: e.dma_start(
                        vspl[tt][:, sp * 512:(sp + 1) * 512], QS[osl][:])],
                        waits=[("ac", S.at("ac"))], sem="o")
                    qs_free[osl] = S.at("o")
            b.ph.run()
        spill_o = S.at("o")
        lnx_stack.close()

        ZB = big.enter_context(nc.sbuf_tensor("b3_zb", [128, NH, T], BF16, side="right"))
        with ExitStack() as st:
            KH = [st.enter_context(nc.sbuf_tensor(f"b3_kh{i}", [128, T], BF16)) for i in range(2)]
            QH = [st.enter_context(nc.sbuf_tensor(f"b3_qh{i}", [128, T], BF16)) for i in range(2)]
            VH = [st.enter_context(nc.sbuf_tensor(f"b3_vh{i}", [128, DC, 128], BF16)) for i in range(2)]
            EB = [st.enter_context(nc.sbuf_tensor(f"b3_eb{i}", [128, TG], BF16)) for i in range(2)]
            Ta = st.enter_context(nc.sbuf_tensor("b3_2ta", [128, TG], F32))
            Tb = st.enter_context(nc.sbuf_tensor("b3_2tb", [128, TG], F32))
            WOb = st.enter_context(nc.sbuf_tensor("b3_wob", [128, DC, D], BF16))
            PS_ = [st.enter_context(nc.psum_tensor(f"b3_ps{i}", [128, TG], F32)) for i in range(2)]
            PZ = st.enter_context(nc.psum_tensor("b3_pz", [128, TG], F32))
            PDen = st.enter_context(nc.psum_tensor("b3_pden", [128, TG], F32))
            b.new_phase()
            vspl3 = vspl.rearrange("tt p o -> p tt o")
            bwo3 = bwo.rearrange("(hec p) o -> p hec o", p=128)
            b.emit_dmas("gpsimd", [lambda e, i=i: e.dma_start(
                WOb[:, 4 * i:4 * (i + 1), :], bwo3[:, 4 * i:4 * (i + 1), :]) for i in range(4)],
                sem="w")
            wo_tag = S.at("w")
            h_free = [0, 0]
            ps_free = [0, 0]
            eb_free = [0, 0]
            z_free = 0
            for h in range(NH):
                hsl = h % 2
                b.emit_dmas("sync",
                            [lambda e, h=h, hsl=hsl: e.dma_start(KH[hsl][:], kspl[h]),
                             lambda e, h=h, hsl=hsl: e.dma_start(QH[hsl][:], qspl[h])] +
                            [lambda e, h=h, hsl=hsl, g=g: e.dma_start(
                                VH[hsl][:, 4 * g:4 * (g + 1), :],
                                vspl3[:, 4 * g:4 * (g + 1), h * 128:(h + 1) * 128]) for g in range(4)],
                            waits=[("pe", h_free[hsl]), ("o", spill_o)], sem="r")
                htag = S.at("r")
                for qc in range(NTG):
                    nkt = 4 * qc + 4
                    for kt in range(nkt):
                        esl = kt % 2
                        boundary = (kt // 4 == qc)
                        off = (kt - 4 * qc) * 128 if boundary else 0
                        N = TG - off
                        def msc(e, hsl=hsl, qc=qc, kt=kt, esl=esl, off=off, N=N):
                            return e.matmul(PS_[esl][:, 0:N], KH[hsl][:, kt * 128:(kt + 1) * 128],
                                            QH[hsl][:, qc * TG + off:(qc + 1) * TG],
                                            start=True, stop=True)
                        b.emit("tensor", msc, waits=[("r", htag), ("ac", ps_free[esl]),
                                                     ("pe", eb_free[esl])], inc="pe")
                        b.emit("scalar", lambda e, esl=esl, N=N: e.activation(
                            EB[esl][:, 0:N], PS_[esl][:, 0:N], AF.Exp,
                            bias=cons["shift_col"], scale=ATTN_SCALE),
                            waits=[("pe", S.at("pe")), ("pe", eb_free[esl])], inc="ac")
                        ps_free[esl] = S.at("ac")
                        mwait = [("ac", S.at("ac"))]
                        if boundary:
                            b.emit("vector", lambda e, esl=esl: e.tensor_tensor(
                                EB[esl][:, 0:128], EB[esl][:, 0:128], tri_t[:], ALU.mult),
                                waits=[("ac", S.at("ac"))], inc="dv")
                            mwait = [("dv", S.at("dv"))]
                        def mzv(e, hsl=hsl, esl=esl, off=off, N=N, kt=kt, nkt=nkt):
                            e.matmul(PZ[:, off:TG], VH[hsl][:, kt, :], EB[esl][:, 0:N],
                                     start=(kt == 0), stop=(kt == nkt - 1))
                            return e.matmul(PDen[:, off:TG], ones_bf[:], EB[esl][:, 0:N],
                                            start=(kt == 0), stop=(kt == nkt - 1))
                        wz = list(mwait)
                        if kt == 0:
                            wz.append(("dv", z_free))
                        b.emit("tensor", mzv, waits=wz, inc="pe")
                        eb_free[esl] = S.at("pe")
                    pet = S.at("pe")
                    def zev(e):
                        e.reciprocal(Ta[:], PDen[:])
                        return e.tensor_tensor(Tb[:], PZ[:], Ta[:], ALU.mult)
                    b.emit("vector", zev, waits=[("pe", pet), ("ac", S.at("ac"))], inc="dv")
                    z_free = S.at("dv")
                    b.emit("scalar", lambda e, h=h, qc=qc: e.activation(
                        ZB[:, h, qc * TG:(qc + 1) * TG], Tb[:], AF.Identity, bias=bbv_t[:, h:h + 1]),
                        waits=[("dv", S.at("dv"))], inc="ac")
                h_free[hsl] = S.at("pe")
            b.emit("vector", lambda e: e.tensor_copy(Ta[:, 0:1], Ta[:, 0:1]),
                   waits=[("ac", S.at("ac"))], inc="dv")
            b.ph.run()
            zb_done = S.at("dv")
            last_o = oproj_accum(b, nc, S, ZB, WOb, bbo_t, src_hbm, dst_hbm,
                                 "b3_op", [zb_done] * NTG, wo_tag)
    return last_o



def build_nc(n_sub=N_SUB):
    nc = bass.Bass("TRN2", target_bir_lowering=False, debug=False)

    xT = nc.dram_tensor("xT", [D, T], F32, kind="ExternalInput").ap()
    awq = nc.dram_tensor("awq", [D, D], F32, kind="ExternalInput").ap()
    awk = nc.dram_tensor("awk", [D, D], F32, kind="ExternalInput").ap()
    awv = nc.dram_tensor("awv", [D, D], F32, kind="ExternalInput").ap()
    awo = nc.dram_tensor("awo", [D, D], F32, kind="ExternalInput").ap()
    abq = nc.dram_tensor("abq", [128, DC], F32, kind="ExternalInput").ap()
    abk = nc.dram_tensor("abk", [1, D], F32, kind="ExternalInput").ap()
    abv = nc.dram_tensor("abv", [1, D], F32, kind="ExternalInput").ap()
    abo = nc.dram_tensor("abo", [128, DC], F32, kind="ExternalInput").ap()
    bwq = nc.dram_tensor("bwq", [D, D], F32, kind="ExternalInput").ap()
    bwk = nc.dram_tensor("bwk", [D, D], F32, kind="ExternalInput").ap()
    bwv = nc.dram_tensor("bwv", [D, D], F32, kind="ExternalInput").ap()
    bwo = nc.dram_tensor("bwo", [D, D], F32, kind="ExternalInput").ap()
    bbq = nc.dram_tensor("bbq", [128, NH], F32, kind="ExternalInput").ap()
    bbk = nc.dram_tensor("bbk", [128, NH], F32, kind="ExternalInput").ap()
    bbv = nc.dram_tensor("bbv", [128, NH], F32, kind="ExternalInput").ap()
    bbo = nc.dram_tensor("bbo", [128, DC], F32, kind="ExternalInput").ap()
    w1T = nc.dram_tensor("w1T", [D, HF], F32, kind="ExternalInput").ap()
    w3T = nc.dram_tensor("w3T", [D, HF], F32, kind="ExternalInput").ap()
    w2cv = nc.dram_tensor("w2cv", [HFC, 128 * D], F32, kind="ExternalInput").ap()
    c_ones = nc.dram_tensor("c_ones", [128, 128], BF16, kind="ExternalInput").ap()
    c_h128 = nc.dram_tensor("c_h128", [128, 128], BF16, kind="ExternalInput").ap()
    c_h64 = nc.dram_tensor("c_h64", [64, 64], BF16, kind="ExternalInput").ap()
    c_tri = nc.dram_tensor("c_tri", [128, 128], BF16, kind="ExternalInput").ap()

    yT = nc.dram_tensor("yT", [D, T], F32, kind="ExternalOutput").ap()

    r1 = nc.dram_tensor("r1", [D, T], F32, kind="Internal").ap()
    r2 = nc.dram_tensor("r2", [D, T], F32, kind="Internal").ap()
    r3 = nc.dram_tensor("r3", [D, T], F32, kind="Internal").ap()
    y1s = nc.dram_tensor("y1s", [HFC, 128, D], BF16, kind="Internal").ap()
    w2p = nc.dram_tensor("w2p", [HFC, 128, D], BF16, kind="Internal").ap()
    qspl = nc.dram_tensor("qspl", [NH, 128, T], BF16, kind="Internal").ap()
    kspl = nc.dram_tensor("kspl", [NH, 128, T], BF16, kind="Internal").ap()
    vspl = nc.dram_tensor("vspl", [DC, 128, D], BF16, kind="Internal").ap()
    hspl = nc.dram_tensor("hspl", [HFC, 128, T], BF16, kind="Internal").ap()

    srcs = [xT, r1, r2, r3]
    dsts = [r1, r2, r3, yT]
    if n_sub < 4:
        dsts[n_sub - 1] = yT

    top = ExitStack()
    S = Sems(nc, top)
    b = B(nc, S)

    ones_bf = top.enter_context(nc.sbuf_tensor("ones_bf", [128, 128], BF16))
    h128_t = top.enter_context(nc.sbuf_tensor("h128_t", [128, 128], BF16))
    tri_t = top.enter_context(nc.sbuf_tensor("tri_t", [128, 128], BF16))
    ones_f = top.enter_context(nc.sbuf_tensor("ones_f", [1, 128], F32))
    abq_t = top.enter_context(nc.sbuf_tensor("abq_t", [128, DC], F32))
    abo_t = top.enter_context(nc.sbuf_tensor("abo_t", [128, DC], F32))
    bbq_t = top.enter_context(nc.sbuf_tensor("bbq_t", [128, NH], F32))
    bbk_t = top.enter_context(nc.sbuf_tensor("bbk_t", [128, NH], F32))
    bbv_t = top.enter_context(nc.sbuf_tensor("bbv_t", [128, NH], F32))
    bbo_t = top.enter_context(nc.sbuf_tensor("bbo_t", [128, DC], F32))

    cst = top.enter_context(nc.sbuf_tensor("cst", [128, 2], F32))

    b.new_phase()
    b.emit_dmas("sync", [lambda e, d=d, s=s: e.dma_start(d[:], s[:]) for d, s in
                         ((ones_bf, c_ones), (h128_t, c_h128), (tri_t, c_tri),
                          (abq_t, abq), (abo_t, abo), (bbq_t, bbq), (bbk_t, bbk), (bbv_t, bbv),
                          (bbo_t, bbo))], sem="r")
    consts_r = S.at("r")
    def init_cst(e):
        e.memset(ones_f[:], 1.0)
        e.memset(cst[:, 0:1], LN_EPS)
        return e.memset(cst[:, 1:2], -EXP_SHIFT)
    b.emit("vector", init_cst, waits=[("r", consts_r)], inc="dv")
    b.ph.run()
    cons = {"ones_bf": ones_bf, "h128_t": h128_t, "tri_t": tri_t, "ones_f": ones_f,
            "eps_col": cst[:, 0:1], "shift_col": cst[:, 1:2], "dv0": S.at("dv")}

    fwht_parts = globals().get("FWHT_PARTS", 2)
    if fwht_parts >= 1:
     with ExitStack() as st:
        RB = [st.enter_context(nc.sbuf_tensor(f"fw_rb{i}", [128, D], BF16)) for i in range(2)]
        AB = [st.enter_context(nc.sbuf_tensor(f"fw_ab{i}", [128, D], BF16)) for i in range(2)]
        PS_ = [st.enter_context(nc.psum_tensor(f"fw_ps{i}", [128, TG], F32)) for i in range(8)]
        b.new_phase()
        w2cv3 = w2cv.rearrange("c (p d) -> c p d", p=128)
        mm_free = [0, 0]
        ev_free = [0, 0]
        st_free = [0, 0]
        for c in range(HFC):
            sl = c % 2
            b.emit_dmas("gpsimd", [lambda e, c=c, sl=sl: e.dma_start(RB[sl][:], w2cv3[c])],
                        waits=[("pe", mm_free[sl])], sem="w")
            wt = S.at("w")
            def mms(e, sl=sl):
                for q in range(4):
                    ins = e.matmul(PS_[sl * 4 + q][:], h128_t[:], RB[sl][:, q * TG:(q + 1) * TG],
                                   start=True, stop=True)
                return ins
            b.emit("tensor", mms, waits=[("w", wt), ("dv", max(ev_free[sl], cons["dv0"])),
                                         ("r", consts_r)], inc="pe")
            mm_free[sl] = S.at("pe")
            def ev(e, sl=sl):
                for q in range(4):
                    ins = e.tensor_scalar_mul(AB[sl][:, q * TG:(q + 1) * TG], PS_[sl * 4 + q][:], W2_SCALE)
                return ins
            b.emit("vector", ev, waits=[("pe", S.at("pe")), ("o", st_free[sl])], inc="dv")
            ev_free[sl] = S.at("dv")
            b.emit_dmas("sync", [lambda e, c=c, sl=sl: e.dma_start(y1s[c], AB[sl][:])],
                        waits=[("dv", S.at("dv"))], sem="o")
            st_free[sl] = S.at("o")
        b.ph.run()
    if fwht_parts >= 2:
     with ExitStack() as st:
        h64_t = st.enter_context(nc.sbuf_tensor("h64_t", [64, 64], BF16))
        RB = [st.enter_context(nc.sbuf_tensor(f"fw2_rb{i}", [64, 4096], BF16)) for i in range(2)]
        AB = [st.enter_context(nc.sbuf_tensor(f"fw2_ab{i}", [64, 4096], BF16)) for i in range(2)]
        PS_ = [st.enter_context(nc.psum_tensor(f"fw2_ps{i}", [64, TG], F32)) for i in range(8)]
        b.new_phase()
        y1cv = y1s.rearrange("c p d -> c (p d)")
        w2pcv = w2p.rearrange("c p d -> c (p d)")
        pass1_o = S.at("o")
        b.emit_dmas("sync", [lambda e: e.dma_start(h64_t[:], c_h64[:])], sem="r")
        h64tag = S.at("r")
        mm_free = [0, 0]
        ev_free = [0, 0]
        st_free = [0, 0]
        prev_ev_dv = 0
        prev_ev_ac = 0
        for ch in range(64):
            sl = ch % 2
            b.emit_dmas("gpsimd", [lambda e, ch=ch, sl=sl: e.dma_start(
                RB[sl][:], y1cv[:, ch * 4096:(ch + 1) * 4096])],
                waits=[("pe", mm_free[sl]), ("o", pass1_o)], sem="w")
            wt = S.at("w")
            def mms(e, sl=sl):
                for q in range(8):
                    ins = e.matmul(PS_[q][:], h64_t[:], RB[sl][:, q * TG:(q + 1) * TG],
                                   start=True, stop=True)
                return ins
            b.emit("tensor", mms, waits=[("r", h64tag), ("w", wt), ("dv", prev_ev_dv),
                                         ("ac", prev_ev_ac)], inc="pe")
            mm_free[sl] = S.at("pe")
            def ev(e, sl=sl):
                for q in range(4):
                    ins = e.tensor_copy(AB[sl][:, q * TG:(q + 1) * TG], PS_[q][:])
                return ins
            b.emit("vector", ev, waits=[("pe", S.at("pe")), ("o", st_free[sl])], inc="dv")
            def ev2(e, sl=sl):
                for q in range(4, 8):
                    ins = e.copy(AB[sl][:, q * TG:(q + 1) * TG], PS_[q][:])
                return ins
            b.emit("scalar", ev2, waits=[("pe", S.at("pe")), ("o", st_free[sl])], inc="ac")
            ev_free[sl] = S.at("dv")
            prev_ev_dv = S.at("dv")
            prev_ev_ac = S.at("ac")
            b.emit_dmas("sync", [lambda e, ch=ch, sl=sl: e.dma_start(
                w2pcv[:, ch * 4096:(ch + 1) * 4096], AB[sl][:])],
                waits=[("dv", S.at("dv")), ("ac", S.at("ac"))], sem="o")
            st_free[sl] = S.at("o")
        b.ph.run()

    for i in range(n_sub):
        kind = (0, 1, 2, 1)[i]
        if kind == 0:
            sub_attn_a(b, nc, S, cons, srcs[i], dsts[i],
                       awq, awk, awv, awo, abq_t, abk, abv, abo_t)
        elif kind == 1:
            sub_ffn(b, nc, S, cons, srcs[i], dsts[i], w1T, w3T, w2p, hspl, f"f{i}")
        else:
            sub_attn_b(b, nc, S, cons, srcs[i], dsts[i],
                       bwq, bwk, bwv, bwo, bbq_t, bbk_t, bbv_t, bbo_t, qspl, kspl, vspl)

    b.new_phase()
    b.emit("sync", lambda e: e.wait_ge(S.h["o"], S.at("o")))
    b.ph.run()
    top.close()
    return nc



def _hadamard(n):
    h = np.array([[1.0]], dtype=np.float64)
    while h.shape[0] < n:
        h = np.block([[h, h], [h, -h]])
    return h


def _prep_shared(inputs):
    f32 = np.float32
    bf = ml_dtypes.bfloat16
    cc = np.ascontiguousarray
    out = {}
    out["awq"] = cc(np.asarray(inputs["L0_attn_a_q_w"], f32).T)
    out["awk"] = cc(np.asarray(inputs["L0_attn_a_k_w"], f32).T)
    out["awv"] = cc(np.asarray(inputs["L0_attn_a_v_w"], f32).T)
    out["awo"] = cc(np.asarray(inputs["L0_attn_a_o_w"], f32).T)
    out["abq"] = cc(np.asarray(inputs["L0_attn_a_q_b"], f32).reshape(DC, 128).T)
    out["abk"] = cc(np.asarray(inputs["L0_attn_a_k_b"], f32).reshape(1, D))
    out["abv"] = cc(np.asarray(inputs["L0_attn_a_v_b"], f32).reshape(1, D))
    out["abo"] = cc(np.asarray(inputs["L0_attn_a_o_b"], f32).reshape(DC, 128).T)
    out["bwq"] = cc(np.asarray(inputs["L0_attn_b_W_Q"], f32).transpose(1, 0, 2).reshape(D, D))
    out["bwk"] = cc(np.asarray(inputs["L0_attn_b_W_K"], f32).transpose(1, 0, 2).reshape(D, D))
    out["bwv"] = cc(np.asarray(inputs["L0_attn_b_W_V"], f32).transpose(1, 0, 2).reshape(D, D))
    out["bwo"] = cc(np.asarray(inputs["L0_attn_b_W_O"], f32).reshape(D, D))
    out["bbq"] = cc(np.asarray(inputs["L0_attn_b_b_Q"], f32).T)
    out["bbk"] = cc(np.asarray(inputs["L0_attn_b_b_K"], f32).T)
    out["bbv"] = cc(np.asarray(inputs["L0_attn_b_b_V"], f32).T)
    out["bbo"] = cc(np.asarray(inputs["L0_attn_b_b_O"], f32).reshape(DC, 128).T)
    out["w1T"] = cc(np.asarray(inputs["L0_ffn_W1"], f32).T)
    out["w3T"] = cc(np.asarray(inputs["L0_ffn_W3"], f32).T)
    out["w2cv"] = cc(np.asarray(inputs["L0_ffn_W2"], f32).T).reshape(HFC, 128 * D)
    out["c_ones"] = np.ones((128, 128), dtype=bf)
    out["c_h128"] = _hadamard(128).astype(bf)
    out["c_h64"] = _hadamard(64).astype(bf)
    tri = (np.arange(128)[:, None] <= np.arange(128)[None, :]).astype(np.float64)
    out["c_tri"] = tri.astype(bf)
    return out


def kernel(**inputs):
    shared = _prep_shared(inputs)
    x = np.asarray(inputs["L0_attn_a_hidden_states"], np.float32)
    nc = build_nc(N_SUB)
    in_maps = []
    for i in range(8):
        m = dict(shared)
        m["xT"] = np.ascontiguousarray(x[i].T)
        in_maps.append(m)
    res = run_bass_kernel_spmd(nc, in_maps, core_ids=list(range(8)))
    outs = [np.ascontiguousarray(np.asarray(r["yT"]).T) for r in res.results]
    return np.stack(outs, axis=0)


# version 04
# speedup vs baseline: 1.0774x; 1.0774x over previous
import numpy as np
import ml_dtypes
from contextlib import ExitStack

import concourse.bass as bass
import concourse.mybir as mybir
from concourse.bass_utils import run_bass_kernel_spmd

F32 = mybir.dt.float32
BF16 = mybir.dt.bfloat16
AF = mybir.ActivationFunctionType
ALU = mybir.AluOpType

D = 2048
T = 2048
DC = 16
NH = 16
HF = 8192
HFC = 64
TG = 512
NTG = 4
LN_EPS = 1e-5
ZA_EPS = 1e-6
EXP_SHIFT = 12.0
ATTN_SCALE = float(1.0 / np.sqrt(128.0))
W2_SCALE = float(1.0 / np.sqrt(8192.0))

ENGS = ("tensor", "vector", "scalar", "gpsimd", "sync")
N_SUB = 4


class Sems:
    def __init__(self, nc, stack):
        self.h = {}
        self.n = {}
        for nm in ("w", "r", "o", "pe", "dv", "ac", "pl"):
            self.h[nm] = stack.enter_context(nc.semaphore("sem_" + nm))
            self.n[nm] = 0

    def at(self, nm):
        return self.n[nm]


class Ph:
    def __init__(self, nc):
        self.nc = nc
        self.q = {e: [] for e in ENGS}

    def run(self):
        with self.nc.Block() as block:
            for eng in ENGS:
                ops = self.q[eng]
                if ops:
                    getattr(block, eng)(lambda e, ops=ops: [f(e) for f in ops])


class B:
    def __init__(self, nc, S):
        self.nc = nc
        self.S = S
        self.ph = None
        self.guard = None
        self.first = None

    def new_phase(self):
        self.ph = Ph(self.nc)
        self.guard = {nm: self.S.at(nm) for nm in ("pe", "dv", "ac", "pl", "r", "w", "o")}
        self.first = {e: True for e in ENGS}
        return self.ph

    def _waits(self, eng, waits):
        S = self.S
        wl = list(waits)
        if self.first[eng]:
            self.first[eng] = False
            wl = list(self.guard.items()) + wl
        agg = {}
        for nm, v in wl:
            if v > 0:
                agg[nm] = max(agg.get(nm, 0), v)
        return tuple((S.h[nm], v) for nm, v in agg.items())

    def emit(self, eng, fn, waits=(), inc=None, k=1):
        S = self.S
        winfo = self._waits(eng, waits)
        h = None
        if inc is not None:
            h = S.h[inc]
            S.n[inc] += k
        def g(e, winfo=winfo, fn=fn, h=h, k=k):
            for hh, v in winfo:
                e.wait_ge(hh, v)
            ins = fn(e)
            if h is not None:
                ins.then_inc(h, k)
        self.ph.q[eng].append(g)

    def emit_dmas(self, eng, fns, waits=(), sem="w"):
        S = self.S
        winfo = self._waits(eng, waits)
        h = S.h[sem]
        S.n[sem] += 16 * len(fns)
        def g(e, winfo=winfo, fns=tuple(fns), h=h):
            for hh, v in winfo:
                e.wait_ge(hh, v)
            for fn in fns:
                fn(e).then_inc(h, 16)
        self.ph.q[eng].append(g)


def _r3(hbm_2d):
    return hbm_2d.rearrange("(dc p) t -> p dc t", p=128)



def ln_phase(b, nc, S, cons, src_hbm, LNX, name):
    ones_bf = cons["ones_bf"]
    with ExitStack() as st:
        XS = [st.enter_context(nc.sbuf_tensor(f"{name}_xs{i}", [128, DC, TG], F32)) for i in range(2)]
        XB = [st.enter_context(nc.sbuf_tensor(f"{name}_xb{i}", [128, TG], BF16)) for i in range(2)]
        XQ = [st.enter_context(nc.sbuf_tensor(f"{name}_xq{i}", [128, TG], BF16)) for i in range(2)]
        M0 = st.enter_context(nc.sbuf_tensor(f"{name}_m0", [128, TG], F32))
        M = [M0, M0]
        MR = [st.enter_context(nc.sbuf_tensor(f"{name}_mr{i}", [128, TG], F32)) for i in range(2)]
        R = [st.enter_context(nc.sbuf_tensor(f"{name}_rr{i}", [128, TG], F32)) for i in range(2)]
        T1 = st.enter_context(nc.sbuf_tensor(f"{name}_t1", [128, TG], F32))
        T2 = st.enter_context(nc.sbuf_tensor(f"{name}_t2", [128, TG], F32))
        P1 = [T1, T2]
        PSS = [st.enter_context(nc.psum_tensor(f"{name}_pss{i}", [128, TG], F32)) for i in range(2)]
        PSQ = [st.enter_context(nc.psum_tensor(f"{name}_psq{i}", [128, TG], F32)) for i in range(2)]
        b.new_phase()
        src3 = _r3(src_hbm)
        apply_done = [0, 0]
        stats_read = [0, 0]
        sub_done = [0, 0]
        sq_read = [0, 0]
        cast_read = [0, 0]
        for tg in range(NTG):
            sl = tg % 2
            b.emit_dmas("sync", [lambda e, sl=sl, tg=tg: e.dma_start(
                XS[sl][:], src3[:, :, tg * TG:(tg + 1) * TG])],
                waits=[("dv", apply_done[sl])], sem="r")
            rtag = S.at("r")
            for dc in range(DC):
                bsl = dc % 2
                def castsq(e, sl=sl, dc=dc, bsl=bsl):
                    e.tensor_copy(XB[bsl][:], XS[sl][:, dc, :])
                    return e.tensor_tensor(XQ[bsl][:], XS[sl][:, dc, :], XS[sl][:, dc, :], ALU.mult)
                b.emit("vector", castsq, waits=[("r", rtag), ("pe", cast_read[bsl])], inc="dv")
                cast_tag = S.at("dv")
                def mm(e, sl=sl, dc=dc, bsl=bsl):
                    e.matmul(PSS[sl][:], ones_bf[:], XB[bsl][:],
                             start=(dc == 0), stop=(dc == DC - 1))
                    return e.matmul(PSQ[sl][:], ones_bf[:], XQ[bsl][:],
                                    start=(dc == 0), stop=(dc == DC - 1))
                wl = [("dv", cast_tag)]
                if dc == 0:
                    wl.append(("dv", stats_read[sl]))
                b.emit("tensor", mm, waits=wl, inc="pe", k=2)
                cast_read[bsl] = S.at("pe")
            petag = S.at("pe")
            def stats1(e, sl=sl):
                e.tensor_scalar_mul(M[sl][:], PSS[sl][:], 1.0 / D)
                e.tensor_scalar_mul(T1[:], PSQ[sl][:], 1.0 / D)
                e.tensor_tensor(T2[:], M[sl][:], M[sl][:], ALU.mult)
                return e.tensor_tensor(T1[:], T1[:], T2[:], ALU.subtract)
            b.emit("vector", stats1, waits=[("pe", petag)], inc="dv")
            stats_read[sl] = S.at("dv")
            b.emit("scalar", lambda e: e.activation(T2[:], T1[:], AF.Sqrt, bias=cons["eps_col"]),
                   waits=[("dv", S.at("dv"))], inc="ac")
            def stats2(e, sl=sl):
                e.reciprocal(R[sl][:], T2[:])
                return e.tensor_tensor(MR[sl][:], M[sl][:], R[sl][:], ALU.mult)
            b.emit("vector", stats2, waits=[("ac", S.at("ac"))], inc="dv")
            dv_stats2 = S.at("dv")
            for dc in range(DC):
                bsl = dc % 2
                b.emit("gpsimd", lambda e, sl=sl, dc=dc, bsl=bsl: e.tensor_tensor(
                    P1[bsl][:], XS[sl][:, dc, :], R[sl][:], ALU.mult),
                    waits=[("dv", max(dv_stats2, sub_done[bsl]))], inc="pl")
                pltag = S.at("pl")
                b.emit("vector", lambda e, sl=sl, dc=dc, bsl=bsl, tg=tg: e.tensor_tensor(
                    LNX[:, dc, tg * TG:(tg + 1) * TG], P1[bsl][:], MR[sl][:], ALU.subtract),
                    waits=[("pl", pltag)], inc="dv")
                sub_done[bsl] = S.at("dv")
            apply_done[sl] = S.at("dv")
        b.ph.run()
    return S.at("dv")


def oproj_accum(b, nc, S, SRC, WO_t, bias_t, src_hbm, dst_hbm, name, src_ready_tags, wo_tag):
    with ExitStack() as st:
        Ta = [st.enter_context(nc.sbuf_tensor(f"{name}_ta{i}", [128, TG], F32)) for i in range(2)]
        OUTT = [st.enter_context(nc.sbuf_tensor(f"{name}_ot{i}", [128, TG], F32)) for i in range(2)]
        RESO = [st.enter_context(nc.sbuf_tensor(f"{name}_rs{i}", [128, 8, TG], F32)) for i in range(2)]
        PO = [st.enter_context(nc.psum_tensor(f"{name}_po{i}", [128, TG], F32)) for i in range(2)]
        b.new_phase()
        src3 = _r3(src_hbm)
        dst3 = _r3(dst_hbm)
        po_free = [0, 0]
        ta_free = [0, 0]
        out_free = [0, 0]
        reso_free = [0, 0]
        reso_tag = [0, 0]
        for tg in range(NTG):
            for half in range(2):
                b.emit_dmas("sync", [lambda e, tg=tg, half=half: e.dma_start(
                    RESO[half][:], src3[:, 8 * half:8 * (half + 1), tg * TG:(tg + 1) * TG])],
                    waits=[("dv", reso_free[half])], sem="r")
                reso_tag[half] = S.at("r")
            for dc in range(DC):
                psl = dc % 2
                hsl = dc // 8
                def mo(e, tg=tg, dc=dc, psl=psl):
                    for hec in range(16):
                        ins = e.matmul(PO[psl][:], WO_t[:, hec, dc * 128:(dc + 1) * 128],
                                       SRC[:, hec, tg * TG:(tg + 1) * TG],
                                       start=(hec == 0), stop=(hec == 15))
                    return ins
                b.emit("tensor", mo,
                       waits=[("w", wo_tag), ("dv", src_ready_tags[tg]), ("ac", po_free[psl])],
                       inc="pe")
                pet = S.at("pe")
                b.emit("scalar", lambda e, dc=dc, psl=psl: e.activation(
                    Ta[psl][:], PO[psl][:], AF.Identity, bias=bias_t[:, dc:dc + 1]),
                    waits=[("pe", pet), ("dv", ta_free[psl])], inc="ac")
                po_free[psl] = S.at("ac")
                actag = S.at("ac")
                osl = dc % 2
                b.emit("vector", lambda e, dc=dc, psl=psl, osl=osl, hsl=hsl: e.tensor_tensor(
                    OUTT[osl][:], Ta[psl][:], RESO[hsl][:, dc % 8, :], ALU.add),
                    waits=[("ac", actag), ("r", reso_tag[hsl]), ("o", out_free[osl])], inc="dv")
                ta_free[psl] = S.at("dv")
                dvt = S.at("dv")
                if dc == 7:
                    reso_free[0] = S.at("dv")
                if dc == 15:
                    reso_free[1] = S.at("dv")
                b.emit_dmas("sync", [lambda e, tg=tg, dc=dc, osl=osl: e.dma_start(
                    dst3[:, dc, tg * TG:(tg + 1) * TG], OUTT[osl][:])],
                    waits=[("dv", dvt)], sem="o")
                out_free[osl] = S.at("o")
        b.ph.run()
    return S.at("o")


def sub_attn_a(b, nc, S, cons, src_hbm, dst_hbm, awq, awk, awv, awo, abq_t, abk, abv, abo_t):
    ones_bf = cons["ones_bf"]
    ones_f = cons["ones_f"]
    with ExitStack() as big:
        kv_bf = big.enter_context(nc.sbuf_tensor("a1_kvbf", [128, NH * 128], BF16, side="right"))
        ksr = big.enter_context(nc.sbuf_tensor("a1_ksr", [128, NH * 128], BF16, side="right"))
        lnx_stack = ExitStack()
        LNX = lnx_stack.enter_context(nc.sbuf_tensor("a1_lnx", [128, DC, T], BF16))
        ln_phase(b, nc, S, cons, src_hbm, LNX, "ln_a1")

        with ExitStack() as st:
            kv_acc = st.enter_context(nc.sbuf_tensor("a1_kvacc", [128, NH * 128], F32))
            ks_acc = st.enter_context(nc.sbuf_tensor("a1_ksacc", [128, NH], F32))
            B_bk = st.enter_context(nc.sbuf_tensor("a1_bbk", [128, D], F32))
            B_bv = st.enter_context(nc.sbuf_tensor("a1_bbv", [128, D], F32))
            rowb = st.enter_context(nc.sbuf_tensor("a1_rowb", [1, D], F32))
            WKs = [st.enter_context(nc.sbuf_tensor(f"a1_wk{i}", [128, DC, 256], BF16)) for i in range(2)]
            WVs = [st.enter_context(nc.sbuf_tensor(f"a1_wv{i}", [128, DC, 256], BF16)) for i in range(2)]
            KPt = [st.enter_context(nc.sbuf_tensor(f"a1_kpt{i}", [128, 256], BF16)) for i in range(2)]
            VTt = [st.enter_context(nc.sbuf_tensor(f"a1_vtt{i}", [128, 256], BF16)) for i in range(2)]
            Ta = st.enter_context(nc.sbuf_tensor("a1_ta", [128, 256], F32))
            Tb = st.enter_context(nc.sbuf_tensor("a1_tb", [128, 256], F32))
            Tc = st.enter_context(nc.sbuf_tensor("a1_tc", [128, 256], F32))
            Td = st.enter_context(nc.sbuf_tensor("a1_td", [128, 128], F32))
            PK = [st.enter_context(nc.psum_tensor(f"a1_pk{i}", [128, TG], F32)) for i in range(2)]
            PV = [st.enter_context(nc.psum_tensor(f"a1_pv{i}", [128, TG], F32)) for i in range(2)]
            PKV = [st.enter_context(nc.psum_tensor(f"a1_pkv{i}", [128, TG], F32)) for i in range(2)]
            PBC = st.enter_context(nc.psum_tensor("a1_pbc", [128, TG], F32))
            b.new_phase()

            b.emit_dmas("sync", [lambda e: e.dma_start(rowb[:], abk[:])], sem="r")
            rt = S.at("r")
            for dst_t, r_src in ((B_bk, abk), (B_bv, abv)):
                for q in range(NTG):
                    b.emit("tensor", lambda e, q=q: e.matmul(
                        PBC[:], ones_f[:], rowb[:, q * TG:(q + 1) * TG], start=True, stop=True),
                        waits=[("r", rt), ("dv", cons["dv0"]), ("ac", S.at("ac"))], inc="pe")
                    b.emit("scalar", lambda e, q=q, dst_t=dst_t: e.copy(
                        dst_t[:, q * TG:(q + 1) * TG], PBC[:]),
                        waits=[("pe", S.at("pe"))], inc="ac")
                if r_src is abk:
                    b.emit_dmas("sync", [lambda e: e.dma_start(rowb[:], abv[:])],
                                waits=[("ac", S.at("ac"))], sem="r")
                    rt = S.at("r")
            bias_ac = S.at("ac")

            def zacc(e):
                e.memset(kv_acc[:], 0.0)
                return e.memset(ks_acc[:], 0.0)
            b.emit("vector", zacc, inc="dv")
            acc_zero = S.at("dv")

            awk3 = awk.rearrange("(dc p) o -> p dc o", p=128)
            awv3 = awv.rearrange("(dc p) o -> p dc o", p=128)
            proj_free = [0, 0]
            kv_free = [0, 0]
            kpt_free = [0, 0]
            w_free = [0, 0]
            pending = None

            def emit_kv_mm(pend):
                h_base, ksl, elu_dv = pend
                pe_tags = []
                for hl in range(2):
                    def mkv(e, hl=hl, ksl=ksl):
                        e.matmul(PKV[hl][:, 0:128], KPt[ksl][:, hl * 128:(hl + 1) * 128],
                                 VTt[ksl][:, hl * 128:(hl + 1) * 128], start=True, stop=True)
                        return e.matmul(PKV[hl][:, 128:129], KPt[ksl][:, hl * 128:(hl + 1) * 128],
                                        ones_bf[:, 0:1], start=True, stop=True)
                    b.emit("tensor", mkv, waits=[("dv", max(elu_dv, kv_free[hl]))], inc="pe")
                    pe_tags.append(S.at("pe"))
                kpt_free[ksl] = S.at("pe")
                return (h_base, pe_tags)

            def emit_kv_acc(pend_mm):
                h_base, pe_tags = pend_mm
                for hl in range(2):
                    h = h_base + hl
                    def acc(e, h=h, hl=hl):
                        e.tensor_tensor(kv_acc[:, h * 128:(h + 1) * 128],
                                        kv_acc[:, h * 128:(h + 1) * 128], PKV[hl][:, 0:128], ALU.add)
                        return e.tensor_tensor(ks_acc[:, h:h + 1], ks_acc[:, h:h + 1],
                                               PKV[hl][:, 128:129], ALU.add)
                    b.emit("vector", acc, waits=[("pe", pe_tags[hl]), ("dv", acc_zero)], inc="dv")
                    kv_free[hl] = S.at("dv")

            it = 0
            for sp in range(8):
                sl = sp % 2
                b.emit_dmas("gpsimd", [
                    lambda e, sp=sp, sl=sl: e.dma_start(WKs[sl][:], awk3[:, :, sp * 256:(sp + 1) * 256]),
                    lambda e, sp=sp, sl=sl: e.dma_start(WVs[sl][:], awv3[:, :, sp * 256:(sp + 1) * 256]),
                ], waits=[("pe", w_free[sl])], sem="w")
                wt = S.at("w")
                for tt in range(DC):
                    psl = tt % 2
                    ksl = it % 2
                    it += 1
                    def mproj(e, sl=sl, tt=tt, psl=psl):
                        for dc in range(DC):
                            e.matmul(PK[psl][:, 0:256], LNX[:, dc, tt * 128:(tt + 1) * 128],
                                     WKs[sl][:, dc, :], start=(dc == 0), stop=(dc == DC - 1))
                            ins = e.matmul(PV[psl][:, 0:256], LNX[:, dc, tt * 128:(tt + 1) * 128],
                                           WVs[sl][:, dc, :], start=(dc == 0), stop=(dc == DC - 1))
                        return ins
                    b.emit("tensor", mproj, waits=[("w", wt), ("dv", proj_free[psl])], inc="pe")
                    if tt == DC - 1:
                        w_free[sl] = S.at("pe")
                    pet = S.at("pe")
                    pend_mm = emit_kv_mm(pending) if pending is not None else None
                    def elu1(e, sp=sp, psl=psl):
                        e.tensor_tensor(Ta[:], PK[psl][:, 0:256], B_bk[:, sp * 256:(sp + 1) * 256], ALU.add)
                        return e.tensor_scalar_min(Tb[:], Ta[:], 0.0)
                    b.emit("vector", elu1, waits=[("pe", pet), ("ac", bias_ac)], inc="dv")
                    b.emit("scalar", lambda e: e.activation(Tc[:], Tb[:], AF.Exp),
                           waits=[("dv", S.at("dv"))], inc="ac")
                    def elu2(e, psl=psl, ksl=ksl):
                        e.tensor_scalar_max(Tb[:], Ta[:], 0.0)
                        e.tensor_tensor(KPt[ksl][:], Tc[:], Tb[:], ALU.add)
                        return e.tensor_copy(VTt[ksl][:], PV[psl][:, 0:256])
                    b.emit("vector", elu2, waits=[("ac", S.at("ac")), ("pe", kpt_free[ksl])], inc="dv")
                    proj_free[psl] = S.at("dv")
                    if pend_mm is not None:
                        emit_kv_acc(pend_mm)
                    pending = (sp * 2, ksl, S.at("dv"))
            emit_kv_acc(emit_kv_mm(pending))
            def fin(e):
                for h in range(NH):
                    e.tensor_scalar_mul(Td[:], B_bv[:, h * 128:(h + 1) * 128], ks_acc[:, h:h + 1])
                    e.tensor_tensor(kv_acc[:, h * 128:(h + 1) * 128],
                                    kv_acc[:, h * 128:(h + 1) * 128], Td[:], ALU.add)
                e.tensor_copy(kv_bf[:], kv_acc[:])
                for h in range(NH):
                    ins = e.tensor_scalar_mul(ksr[:, h * 128:(h + 1) * 128], ones_bf[:],
                                              ks_acc[:, h:h + 1])
                return ins
            b.emit("vector", fin, waits=[("dv", S.at("dv"))], inc="dv")
            b.ph.run()
        kv_ready = S.at("dv")

        QP = big.enter_context(nc.sbuf_tensor("a1_qp", [128, NH, T], BF16, side="right"))
        with ExitStack() as st:
            WQs = [st.enter_context(nc.sbuf_tensor(f"a1_wq{i}", [128, DC, 256], BF16)) for i in range(2)]
            Ta = st.enter_context(nc.sbuf_tensor("a1_qta", [128, TG], F32))
            Tb = st.enter_context(nc.sbuf_tensor("a1_qtb", [128, TG], F32))
            Tc = st.enter_context(nc.sbuf_tensor("a1_qtc", [128, TG], F32))
            PQ = [st.enter_context(nc.psum_tensor(f"a1_pq{i}", [128, TG], F32)) for i in range(2)]
            b.new_phase()
            awq3 = awq.rearrange("(dc p) o -> p dc o", p=128)
            q_free = [0, 0]
            w_free = [0, 0]
            it = 0
            for hp in range(8):
                sl = hp % 2
                b.emit_dmas("gpsimd", [lambda e, hp=hp, sl=sl: e.dma_start(
                    WQs[sl][:], awq3[:, :, hp * 256:(hp + 1) * 256])],
                    waits=[("pe", w_free[sl])], sem="w")
                wt = S.at("w")
                for hl in range(2):
                    hc = hp * 2 + hl
                    for tg in range(NTG):
                        psl = it % 2
                        it += 1
                        def mq(e, sl=sl, hl=hl, tg=tg, psl=psl):
                            for dc in range(DC):
                                ins = e.matmul(PQ[psl][:], WQs[sl][:, dc, hl * 128:(hl + 1) * 128],
                                               LNX[:, dc, tg * TG:(tg + 1) * TG],
                                               start=(dc == 0), stop=(dc == DC - 1))
                            return ins
                        b.emit("tensor", mq, waits=[("w", wt), ("ac", q_free[psl])], inc="pe")
                        if hl == 1 and tg == NTG - 1:
                            w_free[sl] = S.at("pe")
                        b.emit("scalar", lambda e, hc=hc, psl=psl: e.activation(
                            Ta[:], PQ[psl][:], AF.Identity, bias=abq_t[:, hc:hc + 1]),
                            waits=[("pe", S.at("pe")), ("dv", S.at("dv"))], inc="ac")
                        q_free[psl] = S.at("ac")
                        b.emit("vector", lambda e: e.tensor_scalar_min(Tb[:], Ta[:], 0.0),
                               waits=[("ac", S.at("ac"))], inc="dv")
                        b.emit("scalar", lambda e: e.activation(Tc[:], Tb[:], AF.Exp),
                               waits=[("dv", S.at("dv"))], inc="ac")
                        def efin(e, hc=hc, tg=tg):
                            e.tensor_scalar_max(Tb[:], Ta[:], 0.0)
                            return e.tensor_tensor(QP[:, hc, tg * TG:(tg + 1) * TG],
                                                   Tc[:], Tb[:], ALU.add)
                        b.emit("vector", efin, waits=[("ac", S.at("ac"))], inc="dv")
            b.ph.run()

        lnx_stack.close()

        with ExitStack() as st:
            WOa = st.enter_context(nc.sbuf_tensor("a1_woa", [128, DC, D], BF16))
            Ta = st.enter_context(nc.sbuf_tensor("a1_bta", [128, TG], F32))
            Tb = st.enter_context(nc.sbuf_tensor("a1_btb", [128, TG], F32))
            Tc = st.enter_context(nc.sbuf_tensor("a1_btc", [128, TG], F32))
            PA = [st.enter_context(nc.psum_tensor(f"a1_pa{i}", [128, TG], F32)) for i in range(2)]
            PZ = [st.enter_context(nc.psum_tensor(f"a1_pz{i}", [128, TG], F32)) for i in range(2)]
            b.new_phase()
            awo3 = awo.rearrange("(hec p) o -> p hec o", p=128)
            b.emit_dmas("gpsimd", [lambda e, i=i: e.dma_start(
                WOa[:, 4 * i:4 * (i + 1), :], awo3[:, 4 * i:4 * (i + 1), :]) for i in range(4)],
                sem="w")
            wo_tag = S.at("w")
            az_free = [0, 0]
            attn_tg_done = []
            for tg in range(NTG):
                for h in range(NH):
                    psl = (tg * NH + h) % 2
                    def ma(e, h=h, tg=tg, psl=psl):
                        e.matmul(PA[psl][:], kv_bf[:, h * 128:(h + 1) * 128],
                                 QP[:, h, tg * TG:(tg + 1) * TG], start=True, stop=True)
                        return e.matmul(PZ[psl][:], ksr[:, h * 128:(h + 1) * 128],
                                        QP[:, h, tg * TG:(tg + 1) * TG], start=True, stop=True)
                    b.emit("tensor", ma, waits=[("dv", max(kv_ready, az_free[psl]))], inc="pe")
                    def za(e, h=h, tg=tg, psl=psl):
                        e.tensor_scalar_add(Ta[:], PZ[psl][:], ZA_EPS)
                        e.reciprocal(Tb[:], Ta[:])
                        return e.tensor_tensor(QP[:, h, tg * TG:(tg + 1) * TG],
                                               PA[psl][:], Tb[:], ALU.mult)
                    b.emit("vector", za, waits=[("pe", S.at("pe"))], inc="dv")
                    az_free[psl] = S.at("dv")
                attn_tg_done.append(S.at("dv"))
            b.ph.run()
            last_o = oproj_accum(b, nc, S, QP, WOa, abo_t, src_hbm, dst_hbm,
                                 "a1_op", attn_tg_done, wo_tag)
    return last_o


def sub_ffn(b, nc, S, cons, src_hbm, dst_hbm, w1T, w3T, w2p, hspl, name):
    with ExitStack() as big:
        lnx_stack = ExitStack()
        LNX = lnx_stack.enter_context(nc.sbuf_tensor(f"{name}_lnx", [128, DC, T], BF16))
        ln_phase(b, nc, S, cons, src_hbm, LNX, f"ln_{name}")
        with ExitStack() as st:
            W1s = [st.enter_context(nc.sbuf_tensor(f"{name}_w1{i}", [128, DC, 256], BF16)) for i in range(2)]
            W3s = [st.enter_context(nc.sbuf_tensor(f"{name}_w3{i}", [128, DC, 256], BF16)) for i in range(2)]
            SIt = [st.enter_context(nc.sbuf_tensor(f"{name}_si{i}", [128, TG], F32)) for i in range(2)]
            HID = [st.enter_context(nc.sbuf_tensor(f"{name}_hid{i}", [128, T], BF16)) for i in range(2)]
            PU1 = [st.enter_context(nc.psum_tensor(f"{name}_pu1{i}", [128, TG], F32)) for i in range(2)]
            PU3 = [st.enter_context(nc.psum_tensor(f"{name}_pu3{i}", [128, TG], F32)) for i in range(2)]
            b.new_phase()
            w1T3 = w1T.rearrange("(dc p) h -> p dc h", p=128)
            w3T3 = w3T.rearrange("(dc p) h -> p dc h", p=128)
            w_free = [0, 0]
            pu1_free = [0, 0]
            pu3_free = [0, 0]
            si_free = [0, 0]
            hid_free = [0, 0]
            it = 0
            for sp in range(32):
                sl = sp % 2
                b.emit_dmas("gpsimd", [
                    lambda e, sp=sp, sl=sl: e.dma_start(W1s[sl][:], w1T3[:, :, sp * 256:(sp + 1) * 256]),
                    lambda e, sp=sp, sl=sl: e.dma_start(W3s[sl][:], w3T3[:, :, sp * 256:(sp + 1) * 256]),
                ], waits=[("pe", w_free[sl])], sem="w")
                wt = S.at("w")
                for hl in range(2):
                    hc = sp * 2 + hl
                    hsl = hc % 2
                    for tg in range(NTG):
                        psl = it % 2
                        it += 1
                        def mup(e, sl=sl, hl=hl, tg=tg, psl=psl):
                            for dc in range(DC):
                                e.matmul(PU1[psl][:], W1s[sl][:, dc, hl * 128:(hl + 1) * 128],
                                         LNX[:, dc, tg * TG:(tg + 1) * TG],
                                         start=(dc == 0), stop=(dc == DC - 1))
                            for dc in range(DC):
                                ins = e.matmul(PU3[psl][:], W3s[sl][:, dc, hl * 128:(hl + 1) * 128],
                                               LNX[:, dc, tg * TG:(tg + 1) * TG],
                                               start=(dc == 0), stop=(dc == DC - 1))
                            return ins
                        wl = [("w", wt), ("ac", pu1_free[psl]), ("dv", pu3_free[psl])]
                        if tg == 0:
                            wl.append(("o", hid_free[hsl]))
                        b.emit("tensor", mup, waits=wl, inc="pe")
                        if hl == 1 and tg == NTG - 1:
                            w_free[sl] = S.at("pe")
                        b.emit("scalar", lambda e, psl=psl: e.activation(
                            SIt[psl][:], PU1[psl][:], AF.Silu),
                            waits=[("pe", S.at("pe")), ("dv", si_free[psl])], inc="ac")
                        pu1_free[psl] = S.at("ac")
                        b.emit("vector", lambda e, hsl=hsl, tg=tg, psl=psl: e.tensor_tensor(
                            HID[hsl][:, tg * TG:(tg + 1) * TG], SIt[psl][:], PU3[psl][:], ALU.mult),
                            waits=[("ac", S.at("ac"))], inc="dv")
                        pu3_free[psl] = S.at("dv")
                        si_free[psl] = S.at("dv")
                    b.emit_dmas("sync", [lambda e, hc=hc, hsl=hsl: e.dma_start(
                        hspl[hc], HID[hsl][:])],
                        waits=[("dv", S.at("dv"))], sem="o")
                    hid_free[hsl] = S.at("o")
            b.ph.run()
        up_done_o = S.at("o")
        lnx_stack.close()

        with ExitStack() as st:
            HIDT = st.enter_context(nc.sbuf_tensor(f"{name}_hidt", [128, HFC, 1024], BF16))
            W2s = [st.enter_context(nc.sbuf_tensor(f"{name}_w2{i}", [128, 4, 512], BF16)) for i in range(2)]
            RESO = [st.enter_context(nc.sbuf_tensor(f"{name}_rs{i}", [128, 4, 1024], F32)) for i in range(2)]
            OUTT = [st.enter_context(nc.sbuf_tensor(f"{name}_ot{i}", [128, TG], F32)) for i in range(4)]
            PD = [st.enter_context(nc.psum_tensor(f"{name}_pd{i}", [128, TG], F32)) for i in range(8)]
            b.new_phase()
            src3 = _r3(src_hbm)
            dst3 = _r3(dst_hbm)
            hspl3 = hspl.rearrange("hc p t -> p hc t")
            w2p3 = w2p.rearrange("c p d -> p c d")
            hidt_free = 0
            w2_free = [0, 0]
            reso_free = [0, 0]
            out_free = [0, 0, 0, 0]
            pd_free = 0
            for th in range(2):
                wl = [("pe", hidt_free)]
                if th == 0:
                    wl.append(("o", up_done_o))
                b.emit_dmas("sync", [lambda e, th=th, g=g: e.dma_start(
                    HIDT[:, 8 * g:8 * (g + 1), :],
                    hspl3[:, 8 * g:8 * (g + 1), th * 1024:(th + 1) * 1024]) for g in range(8)],
                    waits=wl, sem="r")
                ht = S.at("r")
                for dhq in range(4):
                    rsl = dhq % 2
                    b.emit_dmas("sync", [lambda e, th=th, dhq=dhq, rsl=rsl: e.dma_start(
                        RESO[rsl][:], src3[:, 4 * dhq:4 * (dhq + 1), th * 1024:(th + 1) * 1024])],
                        waits=[("dv", reso_free[rsl])], sem="r")
                    rtag = S.at("r")
                    wt = 0
                    for hc in range(HFC):
                        wsl = (hc // 4) % 2
                        if hc % 4 == 0:
                            b.emit_dmas("gpsimd", [lambda e, dhq=dhq, hc=hc, wsl=wsl: e.dma_start(
                                W2s[wsl][:], w2p3[:, hc:hc + 4, dhq * 512:(dhq + 1) * 512])],
                                waits=[("pe", w2_free[wsl])], sem="w")
                            wt = S.at("w")
                        def mdn(e, hc=hc, wsl=wsl):
                            for dcl in range(4):
                                for nq in range(2):
                                    ins = e.matmul(PD[dcl * 2 + nq][:],
                                                   W2s[wsl][:, hc % 4, dcl * 128:(dcl + 1) * 128],
                                                   HIDT[:, hc, nq * TG:(nq + 1) * TG],
                                                   start=(hc == 0), stop=(hc == HFC - 1))
                            return ins
                        wl = [("w", wt), ("r", ht)]
                        if hc == 0:
                            wl.append(("dv", pd_free))
                        b.emit("tensor", mdn, waits=wl, inc="pe")
                        if hc % 4 == 3:
                            w2_free[wsl] = S.at("pe")
                    pet = S.at("pe")
                    for dcl in range(4):
                        for nq in range(2):
                            osl = (dcl * 2 + nq) % 4
                            b.emit("vector", lambda e, dcl=dcl, nq=nq, rsl=rsl, osl=osl: e.tensor_tensor(
                                OUTT[osl][:], PD[dcl * 2 + nq][:],
                                RESO[rsl][:, dcl, nq * TG:(nq + 1) * TG], ALU.add),
                                waits=[("pe", pet), ("r", rtag), ("o", out_free[osl])], inc="dv")
                            dvt = S.at("dv")
                            b.emit_dmas("sync", [lambda e, th=th, dhq=dhq, dcl=dcl, nq=nq, osl=osl: e.dma_start(
                                dst3[:, dhq * 4 + dcl, th * 1024 + nq * TG:th * 1024 + (nq + 1) * TG],
                                OUTT[osl][:])],
                                waits=[("dv", dvt)], sem="o")
                            out_free[osl] = S.at("o")
                    reso_free[rsl] = S.at("dv")
                    pd_free = S.at("dv")
                hidt_free = S.at("pe")
            b.ph.run()
    return S.at("o")


def sub_attn_b(b, nc, S, cons, src_hbm, dst_hbm, bwq, bwk, bwv, bwo,
               bbq_t, bbk_t, bbv_t, bbo_t, qspl, kspl, vspl):
    ones_bf = cons["ones_bf"]
    tri_t = cons["tri_t"]
    with ExitStack() as big:
        lnx_stack = ExitStack()
        LNX = lnx_stack.enter_context(nc.sbuf_tensor("b3_lnx", [128, DC, T], BF16))
        ln_phase(b, nc, S, cons, src_hbm, LNX, "ln_b3")
        with ExitStack() as st:
            Ws = [st.enter_context(nc.sbuf_tensor(f"b3_w{i}", [128, DC, 256], BF16)) for i in range(2)]
            WVs = [st.enter_context(nc.sbuf_tensor(f"b3_wv{i}", [128, DC, 512], BF16)) for i in range(2)]
            QS = [st.enter_context(nc.sbuf_tensor(f"b3_qs{i}", [128, TG], BF16)) for i in range(2)]
            PQ = [st.enter_context(nc.psum_tensor(f"b3_pq{i}", [128, TG], F32)) for i in range(2)]
            b.new_phase()
            bwq3 = bwq.rearrange("(dc p) o -> p dc o", p=128)
            bwk3 = bwk.rearrange("(dc p) o -> p dc o", p=128)
            bwv3 = bwv.rearrange("(dc p) o -> p dc o", p=128)
            w_free = [0, 0]
            pq_free = [0, 0]
            qs_free = [0, 0]
            it = 0
            for wsrc, spl, bias_t in ((bwq3, qspl, bbq_t), (bwk3, kspl, bbk_t)):
                for hp in range(8):
                    sl = hp % 2
                    b.emit_dmas("gpsimd", [lambda e, wsrc=wsrc, hp=hp, sl=sl: e.dma_start(
                        Ws[sl][:], wsrc[:, :, hp * 256:(hp + 1) * 256])],
                        waits=[("pe", w_free[sl])], sem="w")
                    wt = S.at("w")
                    for hl in range(2):
                        hc = hp * 2 + hl
                        for tg in range(NTG):
                            psl = it % 2
                            osl = it % 2
                            it += 1
                            def mq(e, sl=sl, hl=hl, tg=tg, psl=psl):
                                for dc in range(DC):
                                    ins = e.matmul(PQ[psl][:], Ws[sl][:, dc, hl * 128:(hl + 1) * 128],
                                                   LNX[:, dc, tg * TG:(tg + 1) * TG],
                                                   start=(dc == 0), stop=(dc == DC - 1))
                                return ins
                            b.emit("tensor", mq, waits=[("w", wt), ("ac", pq_free[psl])], inc="pe")
                            if hl == 1 and tg == NTG - 1:
                                w_free[sl] = S.at("pe")
                            b.emit("scalar", lambda e, hc=hc, psl=psl, osl=osl, bias_t=bias_t: e.activation(
                                QS[osl][:], PQ[psl][:], AF.Identity, bias=bias_t[:, hc:hc + 1]),
                                waits=[("pe", S.at("pe")), ("o", qs_free[osl])], inc="ac")
                            pq_free[psl] = S.at("ac")
                            b.emit_dmas("sync", [lambda e, spl=spl, hc=hc, tg=tg, osl=osl: e.dma_start(
                                spl[hc][:, tg * TG:(tg + 1) * TG], QS[osl][:])],
                                waits=[("ac", S.at("ac"))], sem="o")
                            qs_free[osl] = S.at("o")
            for sp in range(4):
                sl = sp % 2
                b.emit_dmas("gpsimd", [lambda e, sp=sp, sl=sl: e.dma_start(
                    WVs[sl][:], bwv3[:, :, sp * 512:(sp + 1) * 512])],
                    waits=[("pe", w_free[sl])], sem="w")
                wt = S.at("w")
                for tt in range(DC):
                    psl = it % 2
                    osl = it % 2
                    it += 1
                    def mv(e, sl=sl, tt=tt, psl=psl):
                        for dc in range(DC):
                            ins = e.matmul(PQ[psl][:], LNX[:, dc, tt * 128:(tt + 1) * 128],
                                           WVs[sl][:, dc, :], start=(dc == 0), stop=(dc == DC - 1))
                        return ins
                    b.emit("tensor", mv, waits=[("w", wt), ("ac", pq_free[psl])], inc="pe")
                    if tt == DC - 1:
                        w_free[sl] = S.at("pe")
                    b.emit("scalar", lambda e, psl=psl, osl=osl: e.copy(QS[osl][:], PQ[psl][:]),
                           waits=[("pe", S.at("pe")), ("o", qs_free[osl])], inc="ac")
                    pq_free[psl] = S.at("ac")
                    b.emit_dmas("sync", [lambda e, tt=tt, sp=sp, osl=osl: e.dma_start(
                        vspl[tt][:, sp * 512:(sp + 1) * 512], QS[osl][:])],
                        waits=[("ac", S.at("ac"))], sem="o")
                    qs_free[osl] = S.at("o")
            b.ph.run()
        spill_o = S.at("o")
        lnx_stack.close()

        ZB = big.enter_context(nc.sbuf_tensor("b3_zb", [128, NH, T], BF16, side="right"))
        with ExitStack() as st:
            KH = [st.enter_context(nc.sbuf_tensor(f"b3_kh{i}", [128, T], BF16)) for i in range(2)]
            QH = [st.enter_context(nc.sbuf_tensor(f"b3_qh{i}", [128, T], BF16)) for i in range(2)]
            VH = [st.enter_context(nc.sbuf_tensor(f"b3_vh{i}", [128, DC, 128], BF16)) for i in range(2)]
            EB = [st.enter_context(nc.sbuf_tensor(f"b3_eb{i}", [128, TG], BF16)) for i in range(4)]
            Ta = st.enter_context(nc.sbuf_tensor("b3_2ta", [128, TG], F32))
            Tb = st.enter_context(nc.sbuf_tensor("b3_2tb", [128, TG], F32))
            Tc = st.enter_context(nc.sbuf_tensor("b3_2tc", [128, TG], F32))
            WOb = st.enter_context(nc.sbuf_tensor("b3_wob", [128, DC, D], BF16))
            PS_ = [st.enter_context(nc.psum_tensor(f"b3_ps{i}", [128, TG], F32)) for i in range(4)]
            PZ = st.enter_context(nc.psum_tensor("b3_pz", [128, TG], F32))
            PDen = st.enter_context(nc.psum_tensor("b3_pden", [128, TG], F32))
            b.new_phase()
            vspl3 = vspl.rearrange("tt p o -> p tt o")
            bwo3 = bwo.rearrange("(hec p) o -> p hec o", p=128)
            b.emit_dmas("gpsimd", [lambda e, i=i: e.dma_start(
                WOb[:, 4 * i:4 * (i + 1), :], bwo3[:, 4 * i:4 * (i + 1), :]) for i in range(4)],
                sem="w")
            wo_tag = S.at("w")
            h_free = [0, 0]
            ps_free = [0, 0, 0, 0]
            eb_free = [0, 0, 0, 0]
            z_free = 0
            for h in range(NH):
                hsl = h % 2
                b.emit_dmas("sync",
                            [lambda e, h=h, hsl=hsl: e.dma_start(KH[hsl][:], kspl[h]),
                             lambda e, h=h, hsl=hsl: e.dma_start(QH[hsl][:], qspl[h])] +
                            [lambda e, h=h, hsl=hsl, g=g: e.dma_start(
                                VH[hsl][:, 4 * g:4 * (g + 1), :],
                                vspl3[:, 4 * g:4 * (g + 1), h * 128:(h + 1) * 128]) for g in range(4)],
                            waits=[("pe", h_free[hsl]), ("o", spill_o)], sem="r")
                htag = S.at("r")
                for qc in range(NTG):
                    nkt = 4 * qc + 4
                    for kt in range(nkt):
                        esl = kt % 4
                        boundary = (kt // 4 == qc)
                        off = (kt - 4 * qc) * 128 if boundary else 0
                        N = TG - off
                        def msc(e, hsl=hsl, qc=qc, kt=kt, esl=esl, off=off, N=N):
                            return e.matmul(PS_[esl][:, 0:N], KH[hsl][:, kt * 128:(kt + 1) * 128],
                                            QH[hsl][:, qc * TG + off:(qc + 1) * TG],
                                            start=True, stop=True)
                        b.emit("tensor", msc, waits=[("r", htag), ("ac", ps_free[esl]),
                                                     ("pe", eb_free[esl])], inc="pe")
                        b.emit("scalar", lambda e, esl=esl, N=N: e.activation(
                            EB[esl][:, 0:N], PS_[esl][:, 0:N], AF.Exp,
                            bias=cons["shift_col"], scale=ATTN_SCALE),
                            waits=[("pe", S.at("pe")), ("pe", eb_free[esl])], inc="ac")
                        ps_free[esl] = S.at("ac")
                        mwait = [("ac", S.at("ac"))]
                        if boundary:
                            b.emit("vector", lambda e, esl=esl: e.tensor_tensor(
                                EB[esl][:, 0:128], EB[esl][:, 0:128], tri_t[:], ALU.mult),
                                waits=[("ac", S.at("ac"))], inc="dv")
                            mwait = [("dv", S.at("dv"))]
                        def mzv(e, hsl=hsl, esl=esl, off=off, N=N, kt=kt, nkt=nkt):
                            e.matmul(PZ[:, off:TG], VH[hsl][:, kt, :], EB[esl][:, 0:N],
                                     start=(kt == 0), stop=(kt == nkt - 1))
                            return e.matmul(PDen[:, off:TG], ones_bf[:], EB[esl][:, 0:N],
                                            start=(kt == 0), stop=(kt == nkt - 1))
                        wz = list(mwait)
                        if kt == 0:
                            wz.append(("dv", z_free))
                        b.emit("tensor", mzv, waits=wz, inc="pe")
                        eb_free[esl] = S.at("pe")
                    pet = S.at("pe")
                    def zev(e):
                        e.reciprocal(Ta[:], PDen[:])
                        return e.tensor_tensor(Tb[:], PZ[:], Ta[:], ALU.mult)
                    b.emit("vector", zev, waits=[("pe", pet), ("ac", S.at("ac"))], inc="dv")
                    z_free = S.at("dv")
                    b.emit("scalar", lambda e, h=h, qc=qc: e.activation(
                        ZB[:, h, qc * TG:(qc + 1) * TG], Tb[:], AF.Identity, bias=bbv_t[:, h:h + 1]),
                        waits=[("dv", S.at("dv"))], inc="ac")
                h_free[hsl] = S.at("pe")
            b.emit("vector", lambda e: e.tensor_copy(Ta[:, 0:1], Ta[:, 0:1]),
                   waits=[("ac", S.at("ac"))], inc="dv")
            b.ph.run()
            zb_done = S.at("dv")
            last_o = oproj_accum(b, nc, S, ZB, WOb, bbo_t, src_hbm, dst_hbm,
                                 "b3_op", [zb_done] * NTG, wo_tag)
    return last_o



def build_nc(n_sub=N_SUB):
    nc = bass.Bass("TRN2", target_bir_lowering=False, debug=False)

    xT = nc.dram_tensor("xT", [D, T], F32, kind="ExternalInput").ap()
    awq = nc.dram_tensor("awq", [D, D], F32, kind="ExternalInput").ap()
    awk = nc.dram_tensor("awk", [D, D], F32, kind="ExternalInput").ap()
    awv = nc.dram_tensor("awv", [D, D], F32, kind="ExternalInput").ap()
    awo = nc.dram_tensor("awo", [D, D], F32, kind="ExternalInput").ap()
    abq = nc.dram_tensor("abq", [128, DC], F32, kind="ExternalInput").ap()
    abk = nc.dram_tensor("abk", [1, D], F32, kind="ExternalInput").ap()
    abv = nc.dram_tensor("abv", [1, D], F32, kind="ExternalInput").ap()
    abo = nc.dram_tensor("abo", [128, DC], F32, kind="ExternalInput").ap()
    bwq = nc.dram_tensor("bwq", [D, D], F32, kind="ExternalInput").ap()
    bwk = nc.dram_tensor("bwk", [D, D], F32, kind="ExternalInput").ap()
    bwv = nc.dram_tensor("bwv", [D, D], F32, kind="ExternalInput").ap()
    bwo = nc.dram_tensor("bwo", [D, D], F32, kind="ExternalInput").ap()
    bbq = nc.dram_tensor("bbq", [128, NH], F32, kind="ExternalInput").ap()
    bbk = nc.dram_tensor("bbk", [128, NH], F32, kind="ExternalInput").ap()
    bbv = nc.dram_tensor("bbv", [128, NH], F32, kind="ExternalInput").ap()
    bbo = nc.dram_tensor("bbo", [128, DC], F32, kind="ExternalInput").ap()
    w1T = nc.dram_tensor("w1T", [D, HF], F32, kind="ExternalInput").ap()
    w3T = nc.dram_tensor("w3T", [D, HF], F32, kind="ExternalInput").ap()
    w2cv = nc.dram_tensor("w2cv", [HFC, 128 * D], F32, kind="ExternalInput").ap()
    c_ones = nc.dram_tensor("c_ones", [128, 128], BF16, kind="ExternalInput").ap()
    c_h128 = nc.dram_tensor("c_h128", [128, 128], BF16, kind="ExternalInput").ap()
    c_h64 = nc.dram_tensor("c_h64", [64, 64], BF16, kind="ExternalInput").ap()
    c_tri = nc.dram_tensor("c_tri", [128, 128], BF16, kind="ExternalInput").ap()

    yT = nc.dram_tensor("yT", [D, T], F32, kind="ExternalOutput").ap()

    r1 = nc.dram_tensor("r1", [D, T], F32, kind="Internal").ap()
    r2 = nc.dram_tensor("r2", [D, T], F32, kind="Internal").ap()
    r3 = nc.dram_tensor("r3", [D, T], F32, kind="Internal").ap()
    y1s = nc.dram_tensor("y1s", [HFC, 128, D], BF16, kind="Internal").ap()
    w2p = nc.dram_tensor("w2p", [HFC, 128, D], BF16, kind="Internal").ap()
    qspl = nc.dram_tensor("qspl", [NH, 128, T], BF16, kind="Internal").ap()
    kspl = nc.dram_tensor("kspl", [NH, 128, T], BF16, kind="Internal").ap()
    vspl = nc.dram_tensor("vspl", [DC, 128, D], BF16, kind="Internal").ap()
    hspl = nc.dram_tensor("hspl", [HFC, 128, T], BF16, kind="Internal").ap()

    srcs = [xT, r1, r2, r3]
    dsts = [r1, r2, r3, yT]
    if n_sub < 4:
        dsts[n_sub - 1] = yT

    top = ExitStack()
    S = Sems(nc, top)
    b = B(nc, S)

    ones_bf = top.enter_context(nc.sbuf_tensor("ones_bf", [128, 128], BF16))
    h128_t = top.enter_context(nc.sbuf_tensor("h128_t", [128, 128], BF16))
    tri_t = top.enter_context(nc.sbuf_tensor("tri_t", [128, 128], BF16))
    ones_f = top.enter_context(nc.sbuf_tensor("ones_f", [1, 128], F32))
    abq_t = top.enter_context(nc.sbuf_tensor("abq_t", [128, DC], F32))
    abo_t = top.enter_context(nc.sbuf_tensor("abo_t", [128, DC], F32))
    bbq_t = top.enter_context(nc.sbuf_tensor("bbq_t", [128, NH], F32))
    bbk_t = top.enter_context(nc.sbuf_tensor("bbk_t", [128, NH], F32))
    bbv_t = top.enter_context(nc.sbuf_tensor("bbv_t", [128, NH], F32))
    bbo_t = top.enter_context(nc.sbuf_tensor("bbo_t", [128, DC], F32))

    cst = top.enter_context(nc.sbuf_tensor("cst", [128, 2], F32))

    b.new_phase()
    b.emit_dmas("sync", [lambda e, d=d, s=s: e.dma_start(d[:], s[:]) for d, s in
                         ((ones_bf, c_ones), (h128_t, c_h128), (tri_t, c_tri),
                          (abq_t, abq), (abo_t, abo), (bbq_t, bbq), (bbk_t, bbk), (bbv_t, bbv),
                          (bbo_t, bbo))], sem="r")
    consts_r = S.at("r")
    def init_cst(e):
        e.memset(ones_f[:], 1.0)
        e.memset(cst[:, 0:1], LN_EPS)
        return e.memset(cst[:, 1:2], -EXP_SHIFT)
    b.emit("vector", init_cst, waits=[("r", consts_r)], inc="dv")
    b.ph.run()
    cons = {"ones_bf": ones_bf, "h128_t": h128_t, "tri_t": tri_t, "ones_f": ones_f,
            "eps_col": cst[:, 0:1], "shift_col": cst[:, 1:2], "dv0": S.at("dv")}

    fwht_parts = globals().get("FWHT_PARTS", 2)
    if fwht_parts >= 1:
     with ExitStack() as st:
        RB = [st.enter_context(nc.sbuf_tensor(f"fw_rb{i}", [128, D], BF16)) for i in range(2)]
        AB = [st.enter_context(nc.sbuf_tensor(f"fw_ab{i}", [128, D], BF16)) for i in range(2)]
        PS_ = [st.enter_context(nc.psum_tensor(f"fw_ps{i}", [128, TG], F32)) for i in range(8)]
        b.new_phase()
        w2cv3 = w2cv.rearrange("c (p d) -> c p d", p=128)
        mm_free = [0, 0]
        ev_free = [0, 0]
        st_free = [0, 0]
        for c in range(HFC):
            sl = c % 2
            b.emit_dmas("gpsimd", [lambda e, c=c, sl=sl: e.dma_start(RB[sl][:], w2cv3[c])],
                        waits=[("pe", mm_free[sl])], sem="w")
            wt = S.at("w")
            def mms(e, sl=sl):
                for q in range(4):
                    ins = e.matmul(PS_[sl * 4 + q][:], h128_t[:], RB[sl][:, q * TG:(q + 1) * TG],
                                   start=True, stop=True)
                return ins
            b.emit("tensor", mms, waits=[("w", wt), ("dv", max(ev_free[sl], cons["dv0"])),
                                         ("r", consts_r)], inc="pe")
            mm_free[sl] = S.at("pe")
            def ev(e, sl=sl):
                for q in range(4):
                    ins = e.tensor_scalar_mul(AB[sl][:, q * TG:(q + 1) * TG], PS_[sl * 4 + q][:], W2_SCALE)
                return ins
            b.emit("vector", ev, waits=[("pe", S.at("pe")), ("o", st_free[sl])], inc="dv")
            ev_free[sl] = S.at("dv")
            b.emit_dmas("sync", [lambda e, c=c, sl=sl: e.dma_start(y1s[c], AB[sl][:])],
                        waits=[("dv", S.at("dv"))], sem="o")
            st_free[sl] = S.at("o")
        b.ph.run()
    if fwht_parts >= 2:
     with ExitStack() as st:
        h64_t = st.enter_context(nc.sbuf_tensor("h64_t", [64, 64], BF16))
        RB = [st.enter_context(nc.sbuf_tensor(f"fw2_rb{i}", [64, 2048], BF16)) for i in range(2)]
        AB = [st.enter_context(nc.sbuf_tensor(f"fw2_ab{i}", [64, 2048], BF16)) for i in range(2)]
        PS_ = [st.enter_context(nc.psum_tensor(f"fw2_ps{i}", [64, TG], F32)) for i in range(8)]
        b.new_phase()
        y1cv = y1s.rearrange("c p d -> c (p d)")
        w2pcv = w2p.rearrange("c p d -> c (p d)")
        pass1_o = S.at("o")
        b.emit_dmas("sync", [lambda e: e.dma_start(h64_t[:], c_h64[:])], sem="r")
        h64tag = S.at("r")
        mm_free = [0, 0]
        ev_free = [0, 0]
        ac_free = [0, 0]
        st_free = [0, 0]
        CW = 2048
        for ch in range(128 * D // CW):
            sl = ch % 2
            b.emit_dmas("gpsimd", [lambda e, ch=ch, sl=sl: e.dma_start(
                RB[sl][:, 0:CW], y1cv[:, ch * CW:(ch + 1) * CW])],
                waits=[("pe", mm_free[sl]), ("o", pass1_o)], sem="w")
            wt = S.at("w")
            def mms(e, sl=sl):
                for q in range(4):
                    ins = e.matmul(PS_[sl * 4 + q][:], h64_t[:], RB[sl][:, q * TG:(q + 1) * TG],
                                   start=True, stop=True)
                return ins
            b.emit("tensor", mms, waits=[("r", h64tag), ("w", wt), ("dv", ev_free[sl]),
                                         ("ac", ac_free[sl])], inc="pe")
            mm_free[sl] = S.at("pe")
            def ev(e, sl=sl):
                for q in range(2):
                    ins = e.tensor_copy(AB[sl][:, q * TG:(q + 1) * TG], PS_[sl * 4 + q][:])
                return ins
            b.emit("vector", ev, waits=[("pe", S.at("pe")), ("o", st_free[sl])], inc="dv")
            def ev2(e, sl=sl):
                for q in range(2, 4):
                    ins = e.copy(AB[sl][:, q * TG:(q + 1) * TG], PS_[sl * 4 + q][:])
                return ins
            b.emit("scalar", ev2, waits=[("pe", S.at("pe")), ("o", st_free[sl])], inc="ac")
            ev_free[sl] = S.at("dv")
            ac_free[sl] = S.at("ac")
            b.emit_dmas("sync", [lambda e, ch=ch, sl=sl: e.dma_start(
                w2pcv[:, ch * CW:(ch + 1) * CW], AB[sl][:, 0:CW])],
                waits=[("dv", S.at("dv")), ("ac", S.at("ac"))], sem="o")
            st_free[sl] = S.at("o")
        b.ph.run()

    for i in range(n_sub):
        kind = (0, 1, 2, 1)[i]
        if kind == 0:
            sub_attn_a(b, nc, S, cons, srcs[i], dsts[i],
                       awq, awk, awv, awo, abq_t, abk, abv, abo_t)
        elif kind == 1:
            sub_ffn(b, nc, S, cons, srcs[i], dsts[i], w1T, w3T, w2p, hspl, f"f{i}")
        else:
            sub_attn_b(b, nc, S, cons, srcs[i], dsts[i],
                       bwq, bwk, bwv, bwo, bbq_t, bbk_t, bbv_t, bbo_t, qspl, kspl, vspl)

    b.new_phase()
    b.emit("sync", lambda e: e.wait_ge(S.h["o"], S.at("o")))
    b.ph.run()
    top.close()
    return nc



def _hadamard(n):
    h = np.array([[1.0]], dtype=np.float64)
    while h.shape[0] < n:
        h = np.block([[h, h], [h, -h]])
    return h


def _prep_shared(inputs):
    f32 = np.float32
    bf = ml_dtypes.bfloat16
    cc = np.ascontiguousarray
    out = {}
    out["awq"] = cc(np.asarray(inputs["L0_attn_a_q_w"], f32).T)
    out["awk"] = cc(np.asarray(inputs["L0_attn_a_k_w"], f32).T)
    out["awv"] = cc(np.asarray(inputs["L0_attn_a_v_w"], f32).T)
    out["awo"] = cc(np.asarray(inputs["L0_attn_a_o_w"], f32).T)
    out["abq"] = cc(np.asarray(inputs["L0_attn_a_q_b"], f32).reshape(DC, 128).T)
    out["abk"] = cc(np.asarray(inputs["L0_attn_a_k_b"], f32).reshape(1, D))
    out["abv"] = cc(np.asarray(inputs["L0_attn_a_v_b"], f32).reshape(1, D))
    out["abo"] = cc(np.asarray(inputs["L0_attn_a_o_b"], f32).reshape(DC, 128).T)
    out["bwq"] = cc(np.asarray(inputs["L0_attn_b_W_Q"], f32).transpose(1, 0, 2).reshape(D, D))
    out["bwk"] = cc(np.asarray(inputs["L0_attn_b_W_K"], f32).transpose(1, 0, 2).reshape(D, D))
    out["bwv"] = cc(np.asarray(inputs["L0_attn_b_W_V"], f32).transpose(1, 0, 2).reshape(D, D))
    out["bwo"] = cc(np.asarray(inputs["L0_attn_b_W_O"], f32).reshape(D, D))
    out["bbq"] = cc(np.asarray(inputs["L0_attn_b_b_Q"], f32).T)
    out["bbk"] = cc(np.asarray(inputs["L0_attn_b_b_K"], f32).T)
    out["bbv"] = cc(np.asarray(inputs["L0_attn_b_b_V"], f32).T)
    out["bbo"] = cc(np.asarray(inputs["L0_attn_b_b_O"], f32).reshape(DC, 128).T)
    out["w1T"] = cc(np.asarray(inputs["L0_ffn_W1"], f32).T)
    out["w3T"] = cc(np.asarray(inputs["L0_ffn_W3"], f32).T)
    out["w2cv"] = cc(np.asarray(inputs["L0_ffn_W2"], f32).T).reshape(HFC, 128 * D)
    out["c_ones"] = np.ones((128, 128), dtype=bf)
    out["c_h128"] = _hadamard(128).astype(bf)
    out["c_h64"] = _hadamard(64).astype(bf)
    tri = (np.arange(128)[:, None] <= np.arange(128)[None, :]).astype(np.float64)
    out["c_tri"] = tri.astype(bf)
    return out


def kernel(**inputs):
    shared = _prep_shared(inputs)
    x = np.asarray(inputs["L0_attn_a_hidden_states"], np.float32)
    nc = build_nc(N_SUB)
    in_maps = []
    for i in range(8):
        m = dict(shared)
        m["xT"] = np.ascontiguousarray(x[i].T)
        in_maps.append(m)
    res = run_bass_kernel_spmd(nc, in_maps, core_ids=list(range(8)))
    outs = [np.ascontiguousarray(np.asarray(r["yT"]).T) for r in res.results]
    return np.stack(outs, axis=0)
